# version 05
# baseline (speedup 1.0000x reference)
import numpy as np
from contextlib import ExitStack

import concourse.bass as bass
import concourse.mybir as mybir
from concourse.bass_utils import run_bass_kernel_spmd

F32 = mybir.dt.float32
BF16 = mybir.dt.bfloat16
U8 = mybir.dt.uint8
AF = mybir.ActivationFunctionType
ALU = mybir.AluOpType
AXX = mybir.AxisListType.X

C = 1024
T = 2048
G = C // 128
NSLAB = 4
SLAB = T // NSLAB
N_CORES = 8
LN_EPS = 1e-5


def pack_lhsT(M):
    cin, dout = M.shape
    KC, DG = cin // 128, dout // 128
    return np.ascontiguousarray(
        np.asarray(M).reshape(KC, 128, DG, 128).transpose(1, 2, 0, 3)).astype(np.float32)


def pack_w(W):
    return pack_lhsT(np.ascontiguousarray(np.asarray(W).T))


def vec_fm(v):
    return np.ascontiguousarray(np.asarray(v).reshape(-1, 128).T).astype(np.float32)


def x_fm(xbt):
    xbt = np.asarray(xbt)
    Tc = xbt.shape[0]
    return np.ascontiguousarray(xbt.reshape(Tc, -1, 128).transpose(1, 2, 0)).astype(np.float32)


def x_unfm(xg):
    xg = np.asarray(xg)
    return np.ascontiguousarray(xg.transpose(2, 0, 1).reshape(xg.shape[2], -1))



CHAINED = {"vector": "v", "scalar": "s", "gpsimd": "g"}


class Prog:
    ENGINES = ("sync", "scalar", "vector", "tensor", "gpsimd")

    def __init__(self, nc):
        self.nc = nc
        self.q = {e: [] for e in self.ENGINES}
        self.sem_names = []
        self.sem_cnt = {}
        self.sem_h = {}
        self.last_wait = {}

    def sem(self, name):
        if name not in self.sem_cnt:
            self.sem_names.append(name)
            self.sem_cnt[name] = 0
        return name

    def _h(self, name):
        return self.sem_h[name]

    def emit(self, eng, fn):
        self.q[eng].append(fn)

    def dma(self, eng, out, in_, sem):
        self.sem(sem)
        self.sem_cnt[sem] += 16
        tgt = self.sem_cnt[sem]
        self.emit(eng, lambda e, out=out, in_=in_, s=sem:
                  e.dma_start(out=out, in_=in_).then_inc(self._h(s), 16))
        return tgt

    def wait(self, eng, sem, val):
        self.sem(sem)
        if val <= 0:
            return
        key = (eng, sem)
        if self.last_wait.get(key, -1) >= val:
            return
        self.last_wait[key] = val
        self.emit(eng, lambda e, s=sem, v=val: e.wait_ge(self._h(s), v))

    def op(self, eng, method, *args, sem=None, inc=1, **kwargs):
        ch = CHAINED.get(eng)
        if ch is not None:
            self.sem(ch)
            prev = self.sem_cnt[ch]
            self.wait(eng, ch, prev)
            self.sem_cnt[ch] += 1
            tgt = self.sem_cnt[ch]
            assert sem is None or sem == ch, f"chained engine {eng} must use sem {ch}"

            def fnc(e, m=method, a=args, k=kwargs, s=ch):
                getattr(e, m)(*a, **k).then_inc(self._h(s), 1)
            self.emit(eng, fnc)
            return tgt
        tgt = None
        if sem is not None:
            self.sem(sem)
            self.sem_cnt[sem] += inc
            tgt = self.sem_cnt[sem]

        def fn(e, m=method, a=args, k=kwargs, s=sem, i=inc):
            r = getattr(e, m)(*a, **k)
            if s is not None:
                r.then_inc(self._h(s), i)
        self.emit(eng, fn)
        return tgt

    def cur(self, sem):
        self.sem(sem)
        return self.sem_cnt[sem]

    def wait_all(self, eng, base, n):
        for i in range(n):
            self.wait(eng, f"{base}{i}", self.cur(f"{base}{i}"))

    def build(self, es):
        nc = self.nc
        for name in self.sem_names:
            self.sem_h[name] = es.enter_context(nc.semaphore(name))
        block = es.enter_context(nc.Block())
        for eng_name in self.ENGINES:
            ops = self.q[eng_name]
            if not ops:
                continue
            deco = getattr(block, eng_name)

            def section(eng, ops=ops):
                for fn in ops:
                    fn(eng)
            deco(section)


class Banks:

    def __init__(self, p, tiles, ids):
        self.p = p
        self.tiles = tiles
        self.ids = list(ids)
        self.guard = {i: None for i in self.ids}
        self.rr = 0

    def take(self, eng):
        i = self.ids[self.rr % len(self.ids)]
        self.rr += 1
        gv = self.guard[i]
        if gv is not None:
            self.p.wait(eng, gv[0], gv[1])
        return i, self.tiles[i]

    def set_guard(self, i, sem, val):
        self.guard[i] = (sem, val)



def alloc_common(nc, es, want_X=True, n_hs=G, n_psum=8):
    b = {}
    if want_X:
        b["X"] = [es.enter_context(nc.sbuf_tensor(f"X{g}", [128, T], F32)) for g in range(G)]
    b["Hs"] = [es.enter_context(nc.sbuf_tensor(f"Hs{g}", [128, SLAB], F32)) for g in range(n_hs)]
    b["xb"] = [es.enter_context(nc.sbuf_tensor(f"xbs{i}", [128, SLAB], BF16)) for i in range(2)]
    b["xsq"] = [es.enter_context(nc.sbuf_tensor(f"xsq{i}", [128, SLAB], BF16)) for i in range(2)]
    b["onesb"] = es.enter_context(nc.sbuf_tensor("onesb", [128, 128], BF16))
    b["epsc"] = es.enter_context(nc.sbuf_tensor("epsc", [128, 1], F32))
    b["colb"] = es.enter_context(nc.sbuf_tensor("colb", [128, 16], BF16))
    b["colq"] = es.enter_context(nc.sbuf_tensor("colq", [128, 16], BF16))
    b["stat"] = [es.enter_context(nc.sbuf_tensor(f"stat{i}", [128, SLAB], F32)) for i in range(3)]
    b["psum"] = [es.enter_context(nc.psum_tensor(f"ps{i}", [128, 512], F32)) for i in range(n_psum)]
    return b


def emit_load_x(p, b, xin_ap):
    for g in range(G):
        p.dma("sync", b["X"][g][:], xin_ap[g], "dx")
    return p.cur("dx")


class WeightLoader:

    def __init__(self, nc, es, p, name, stage_cols=512):
        self.p = p
        self.name = name
        self.stage_cols = stage_cols
        self.stg = [es.enter_context(nc.sbuf_tensor(f"wstg{name}{i}", [128, stage_cols], F32))
                    for i in range(2)]
        self.seq = 0
        self.cast_tgts = []

    def load(self, hbm_ap, dst, dst_off, n_cols, ld_eng="sync", cast_eng="vector",
             negate=False):
        p = self.p
        sc = self.stage_cols
        csem = CHAINED[cast_eng]
        nchunks = (n_cols + sc - 1) // sc
        tgt = 0
        for i in range(nchunks):
            par = self.seq % 2
            lsem = f"wl{self.name}{par}"
            cols = min(sc, n_cols - i * sc)
            if self.seq >= 2:
                p.wait(ld_eng, csem, self.cast_tgts[self.seq - 2])
            ld_tgt = p.dma(ld_eng, self.stg[par][:, 0:cols],
                           hbm_ap[:, i * sc:i * sc + cols], lsem)
            p.wait(cast_eng, lsem, ld_tgt)
            dview = dst[:, dst_off + i * sc: dst_off + i * sc + cols]
            if negate:
                tgt = p.op(cast_eng, "tensor_scalar", dview, self.stg[par][:, 0:cols],
                           -1.0, None, ALU.mult)
            else:
                tgt = p.op(cast_eng, "tensor_copy", dview, self.stg[par][:, 0:cols])
            self.cast_tgts.append(tgt)
            self.seq += 1
        return tgt


def emit_ln_slab(p, b, s, out_fp32=None, out_bf16=None, src=None, gamma=None, beta=None,
                 src_is_slab=False, width=SLAB, out_col0=0):
    ts = slice(0, width) if src_is_slab else slice(s * SLAB, s * SLAB + width)
    src = src or b["X"]
    psA, psB = b["psum"][0], b["psum"][1]
    tentry = p.cur("t")
    p.wait("vector", "t", tentry)
    p.wait("scalar", "t", tentry)
    vgate, sgate, tA, tB = [], [], [], []
    W = width
    oc = slice(out_col0, out_col0 + W)
    for g in range(G):
        par = g % 2
        if g >= 2:
            p.wait("vector", "t", tA[g - 2])
            p.wait("vector", "s", sgate[g - 2])
        vgate.append(p.op("vector", "tensor_copy", b["xb"][par][:, 0:W], src[g][:, ts]))
        if g >= 2:
            p.wait("scalar", "t", tB[g - 2])
        p.wait("scalar", "v", vgate[g])
        sgate.append(p.op("scalar", "activation", b["xsq"][par][:, 0:W],
                          b["xb"][par][:, 0:W], AF.Square))
        p.wait("tensor", "v", vgate[g])
        tA.append(p.op("tensor", "matmul", psA[:, 0:W], b["onesb"][:], b["xb"][par][:, 0:W],
                       start=(g == 0), stop=(g == G - 1), sem="t"))
        p.wait("tensor", "s", sgate[g])
        tB.append(p.op("tensor", "matmul", psB[:, 0:W], b["onesb"][:], b["xsq"][par][:, 0:W],
                       start=(g == 0), stop=(g == G - 1), sem="t"))
    tstat = p.cur("t")
    m, var, rstd = b["stat"]
    msq = b["Hs"][0]
    p.wait("vector", "t", tstat)
    p.op("vector", "tensor_scalar", m[:, 0:W], psA[:, 0:W], 1.0 / C, None, ALU.mult)
    p.op("vector", "tensor_tensor", msq[:, 0:W], m[:, 0:W], m[:, 0:W], ALU.mult)
    vvar = p.op("vector", "scalar_tensor_tensor", var[:, 0:W], psB[:, 0:W], 1.0 / C,
                msq[:, 0:W], ALU.mult, ALU.subtract)
    p.wait("scalar", "v", vvar)
    p.op("scalar", "activation", msq[:, 0:W], var[:, 0:W], AF.Ln, bias=b["epsc"][:, 0:1])
    srstd = p.op("scalar", "activation", rstd[:, 0:W], msq[:, 0:W], AF.Exp, scale=-0.5)
    p.wait("vector", "s", srstd)
    last = 0
    for g in range(G):
        hs = b["Hs"][g % len(b["Hs"])]
        p.op("vector", "tensor_tensor", hs[:, 0:W], src[g][:, ts], m[:, 0:W],
             ALU.subtract)
        dsts = []
        if out_fp32 is not None:
            dsts.append(out_fp32[g])
        if out_bf16 is not None:
            dsts.append(out_bf16[g])
        if gamma is not None:
            p.op("vector", "tensor_tensor", var[:, 0:W], hs[:, 0:W], rstd[:, 0:W],
                 ALU.mult)
            for d in dsts:
                last = p.op("vector", "tensor_scalar", d[:, oc], var[:, 0:W], gamma[g],
                            beta[g], ALU.mult, ALU.add)
        else:
            first = dsts[0]
            last = p.op("vector", "tensor_tensor", first[:, oc], hs[:, 0:W],
                        rstd[:, 0:W], ALU.mult)
            for d in dsts[1:]:
                last = p.op("vector", "tensor_copy", d[:, oc], first[:, oc])
    return last


def emit_ln_cols(p, b, src, out, ncol, mask_col=None):
    tb, tq = b["colb"], b["colq"]
    psA, psB = b["psum"][0], b["psum"][1]
    vc = p.op("vector", "tensor_copy", tb[:, 0:ncol], src[:, 0:ncol])
    p.wait("scalar", "v", vc)
    sq = p.op("scalar", "activation", tq[:, 0:ncol], tb[:, 0:ncol], AF.Square)
    p.wait("tensor", "v", vc)
    p.wait("tensor", "s", sq)
    p.op("tensor", "matmul", psA[:, 0:ncol], b["onesb"][:], tb[:, 0:ncol],
         start=True, stop=True, sem="t")
    p.op("tensor", "matmul", psB[:, 0:ncol], b["onesb"][:], tq[:, 0:ncol],
         start=True, stop=True, sem="t")
    tst = p.cur("t")
    m, var, rstd = b["stat"]
    p.wait("vector", "t", tst)
    p.op("vector", "reduce_sum", rstd[:, 2:3], psA[:, 0:ncol], axis=AXX)
    p.op("vector", "reduce_sum", var[:, 1:2], psB[:, 0:ncol], axis=AXX)
    p.op("vector", "tensor_scalar", m[:, 0:1], rstd[:, 2:3], 1.0 / C, None, ALU.mult)
    p.op("vector", "tensor_tensor", rstd[:, 3:4], m[:, 0:1], m[:, 0:1], ALU.mult)
    vv = p.op("vector", "scalar_tensor_tensor", var[:, 0:1], var[:, 1:2], 1.0 / C,
              rstd[:, 3:4], ALU.mult, ALU.subtract)
    p.wait("scalar", "v", vv)
    p.op("scalar", "activation", rstd[:, 1:2], var[:, 0:1], AF.Ln, bias=b["epsc"][:, 0:1])
    sr = p.op("scalar", "activation", rstd[:, 0:1], rstd[:, 1:2], AF.Exp, scale=-0.5)
    p.wait("vector", "s", sr)
    if mask_col is not None:
        p.op("vector", "tensor_scalar", var[:, 2:3], rstd[:, 0:1], mask_col, None, ALU.mult)
        last = p.op("vector", "tensor_scalar", out[:, 0:ncol], src[:, 0:ncol],
                    m[:, 0:1], var[:, 2:3], ALU.subtract, ALU.mult)
    else:
        last = p.op("vector", "tensor_scalar", out[:, 0:ncol], src[:, 0:ncol],
                    m[:, 0:1], rstd[:, 0:1], ALU.subtract, ALU.mult)
    return last



def build_ffn01():
    nc = bass.Bass("TRN2", target_bir_lowering=False)
    xin = nc.dram_tensor("xin", [G, 128, T], F32, kind="ExternalInput").ap()
    xtail = nc.dram_tensor("xtail", [128, G], F32, kind="ExternalInput").ap()
    w_r = nc.dram_tensor("w_r", [128, G, G, 128], F32, kind="ExternalInput").ap()
    w_k = nc.dram_tensor("w_k", [128, G, G, 128], F32, kind="ExternalInput").ap()
    w_v = nc.dram_tensor("w_v", [128, G, G, 128], F32, kind="ExternalInput").ap()
    mu_r = nc.dram_tensor("mu_r", [128, G], F32, kind="ExternalInput").ap()
    mu_k = nc.dram_tensor("mu_k", [128, G], F32, kind="ExternalInput").ap()
    has_tail = nc.dram_tensor("has_tail", [128, 1], F32, kind="ExternalInput").ap()
    xout = nc.dram_tensor("xout", [G, 128, T], F32, kind="ExternalOutput").ap()

    es = ExitStack()
    with es:
        p = Prog(nc)
        b = alloc_common(nc, es)
        WR = es.enter_context(nc.sbuf_tensor("WR", [128, G * G * 128], BF16))
        WK = es.enter_context(nc.sbuf_tensor("WK", [128, G * G * 128], BF16))
        WV = es.enter_context(nc.sbuf_tensor("WV", [128, G * G * 128], BF16))
        mur = es.enter_context(nc.sbuf_tensor("mur", [128, G], F32))
        muk = es.enter_context(nc.sbuf_tensor("muk", [128, G], F32))
        tailf = es.enter_context(nc.sbuf_tensor("tailf", [128, G], F32))
        hasT = es.enter_context(nc.sbuf_tensor("hasT", [128, 1], F32))
        XRb = [es.enter_context(nc.sbuf_tensor(f"XRb{g}", [128, SLAB], BF16)) for g in range(G)]
        XKb = [es.enter_context(nc.sbuf_tensor(f"XKb{g}", [128, SLAB], BF16)) for g in range(G)]
        Rf = [es.enter_context(nc.sbuf_tensor(f"Rf{g}", [128, SLAB], F32)) for g in range(G)]
        KA = [es.enter_context(nc.sbuf_tensor(f"KA{g}", [128, SLAB], BF16)) for g in range(G)]
        rk = [es.enter_context(nc.sbuf_tensor("rkf0", [128, SLAB], F32))] * 2
        dmix = es.enter_context(nc.sbuf_tensor("dmix", [128, SLAB], F32))
        HL = [es.enter_context(nc.sbuf_tensor(f"HL{g}", [128, SLAB], F32)) for g in range(G)]
        Hcol = es.enter_context(nc.sbuf_tensor("Hcol", [128, G], F32))
        Hcol2 = es.enter_context(nc.sbuf_tensor("Hcol2", [128, G], F32))
        vt = dmix
        Xn = [es.enter_context(nc.sbuf_tensor(f"Xn{i}", [128, SLAB], F32)) for i in range(2)]

        p.op("vector", "memset", b["onesb"][:], 1.0)
        p.op("vector", "memset", b["epsc"][:], LN_EPS)
        xld = emit_load_x(p, b, xin)
        p.dma("sync", mur[:], mu_r[:, :], "dsm")
        p.dma("sync", muk[:], mu_k[:, :], "dsm")
        p.dma("sync", hasT[:], has_tail[:, :], "dsm")
        p.dma("sync", tailf[:], xtail[:, :], "dsm")
        sm_ld = p.cur("dsm")
        wl = WeightLoader(nc, es, p, "w")
        for W_ap, Wb in ((w_r, WR), (w_k, WK), (w_v, WV)):
            wdone = wl.load(W_ap.rearrange("p a b c -> p (a b c)"), Wb, 0, G * G * 128)

        p.wait("vector", "dsm", sm_ld)
        p.wait("vector", "dx", xld)
        emit_ln_cols(p, b, tailf, Hcol, G, mask_col=hasT[:, 0:1])

        banks = Banks(p, b["psum"], range(2, 8))
        xo_tgts = []
        for s in range(NSLAB):
            ts = slice(s * SLAB, (s + 1) * SLAB)
            Hc = Hcol if s % 2 == 0 else Hcol2
            Hc_next = Hcol2 if s % 2 == 0 else Hcol
            emit_ln_slab(p, b, s, out_fp32=HL)
            for g in range(G):
                p.op("vector", "tensor_tensor", dmix[:, 1:SLAB], HL[g][:, 1:SLAB],
                     HL[g][:, 0:SLAB - 1], ALU.subtract)
                p.op("vector", "tensor_tensor", dmix[:, 0:1], HL[g][:, 0:1],
                     Hc[:, g:g + 1], ALU.subtract)
                p.op("vector", "scalar_tensor_tensor", XRb[g][:, 1:SLAB], dmix[:, 1:SLAB],
                     mur[:, g:g + 1], HL[g][:, 0:SLAB - 1], ALU.mult, ALU.add)
                p.op("vector", "scalar_tensor_tensor", XRb[g][:, 0:1], dmix[:, 0:1],
                     mur[:, g:g + 1], Hc[:, g:g + 1], ALU.mult, ALU.add)
                p.op("vector", "scalar_tensor_tensor", XKb[g][:, 1:SLAB], dmix[:, 1:SLAB],
                     muk[:, g:g + 1], HL[g][:, 0:SLAB - 1], ALU.mult, ALU.add)
                p.op("vector", "scalar_tensor_tensor", XKb[g][:, 0:1], dmix[:, 0:1],
                     muk[:, g:g + 1], Hc[:, g:g + 1], ALU.mult, ALU.add)
            for g in range(G):
                p.op("vector", "tensor_copy", Hc_next[:, g:g + 1], HL[g][:, SLAB - 1:SLAB])
            vmix = p.cur("v")

            p.wait("tensor", "v", vmix)
            if s == 0:
                p.wait("tensor", "v", wdone)
            for dg in range(G):
                bi, bank = banks.take("tensor")
                for k in range(G):
                    p.op("tensor", "matmul", bank[:],
                         WR[:, (dg * G + k) * 128:(dg * G + k + 1) * 128],
                         XRb[k][:], start=(k == 0), stop=(k == G - 1),
                         sem=("t" if k == G - 1 else None))
                tmm = p.cur("t")
                p.wait("scalar", "t", tmm)
                ev = p.op("scalar", "activation", Rf[dg][:], bank[:], AF.Sigmoid)
                banks.set_guard(bi, "s", ev)
            vka = []
            for dg in range(G):
                bi, bank = banks.take("tensor")
                for k in range(G):
                    p.op("tensor", "matmul", bank[:],
                         WK[:, (dg * G + k) * 128:(dg * G + k + 1) * 128],
                         XKb[k][:], start=(k == 0), stop=(k == G - 1),
                         sem=("t" if k == G - 1 else None))
                tmm = p.cur("t")
                par = dg % 2
                p.wait("scalar", "t", tmm)
                if dg >= 1:
                    p.wait("scalar", "v", vka[dg - 1])
                evs = p.op("scalar", "activation", rk[par][:], bank[:], AF.Relu)
                banks.set_guard(bi, "s", evs)
                p.wait("vector", "s", evs)
                vka.append(p.op("vector", "tensor_tensor", KA[dg][:], rk[par][:], rk[par][:],
                                ALU.mult))
            p.wait("tensor", "v", vka[-1])
            for dg in range(G):
                bi, bank = banks.take("tensor")
                for k in range(G):
                    p.op("tensor", "matmul", bank[:],
                         WV[:, (dg * G + k) * 128:(dg * G + k + 1) * 128],
                         KA[k][:], start=(k == 0), stop=(k == G - 1),
                         sem=("t" if k == G - 1 else None))
                tmm = p.cur("t")
                p.wait("vector", "t", tmm)
                xp = len(xo_tgts) % 2
                if len(xo_tgts) >= 2:
                    p.wait("vector", f"dso{xp}", xo_tgts[-2])
                p.op("vector", "tensor_tensor", vt[:], bank[:], Rf[dg][:], ALU.mult)
                evv = p.op("vector", "scalar_tensor_tensor", Xn[xp][:], vt[:], 0.5,
                           b["X"][dg][:, ts], ALU.mult, ALU.add)
                banks.set_guard(bi, "v", evv)
                p.wait("gpsimd", "v", evv)
                xo_tgts.append(p.dma("gpsimd", xout[dg][:, ts], Xn[xp][:], f"dso{xp}"))
        p.wait_all("gpsimd", "dso", 2)
        p.build(es)
    return nc


def emit_proj(p, banks, W, rhs, evac, kcs=G, dgs=G, wait_v=None):
    if wait_v is not None:
        p.wait("tensor", "v", wait_v)
    for dg in range(dgs):
        bi, bank = banks.take("tensor")
        for k in range(kcs):
            p.op("tensor", "matmul", bank[:],
                 W[:, (dg * kcs + k) * 128:(dg * kcs + k + 1) * 128],
                 rhs[k][:], start=(k == 0), stop=(k == kcs - 1),
                 sem=("t" if k == kcs - 1 else None))
        evac(dg, bi, bank, p.cur("t"))



def build_l0a1():
    nc = bass.Bass("TRN2", target_bir_lowering=False)
    xin = nc.dram_tensor("xin", [G, 128, T], F32, kind="ExternalInput").ap()
    w_q = nc.dram_tensor("w_q", [128, G, G, 128], F32, kind="ExternalInput").ap()
    w_k = nc.dram_tensor("w_k", [128, G, G, 128], F32, kind="ExternalInput").ap()
    w_v = nc.dram_tensor("w_v", [128, G, G, 128], F32, kind="ExternalInput").ap()
    w_0 = nc.dram_tensor("w_0", [128, G, G, 128], F32, kind="ExternalInput").ap()
    a_i = nc.dram_tensor("a_i", [128, G, 8], F32, kind="ExternalInput").ap()
    b_i = nc.dram_tensor("b_i", [8, G, 128], F32, kind="ExternalInput").ap()
    ln_g = nc.dram_tensor("ln_g", [128, G], F32, kind="ExternalInput").ap()
    ln_b = nc.dram_tensor("ln_b", [128, G], F32, kind="ExternalInput").ap()
    q_st = nc.dram_tensor("q_st", [G, 128, T], F32, kind="ExternalOutput").ap()
    y0_st = nc.dram_tensor("y0_st", [G, 128, T], F32, kind="ExternalOutput").ap()
    cum_st = nc.dram_tensor("cum_st", [G, 128, T], F32, kind="ExternalOutput").ap()
    gsum = nc.dram_tensor("gsum", [128, G], F32, kind="ExternalOutput").ap()

    es = ExitStack()
    with es:
        p = Prog(nc)
        b = alloc_common(nc, es)
        WQ = es.enter_context(nc.sbuf_tensor("WQ", [128, G * G * 128], BF16))
        WK = es.enter_context(nc.sbuf_tensor("WK", [128, G * G * 128], BF16))
        WV = es.enter_context(nc.sbuf_tensor("WV", [128, G * G * 128], BF16))
        W0 = es.enter_context(nc.sbuf_tensor("W0", [128, G * G * 128], BF16))
        AI = es.enter_context(nc.sbuf_tensor("AI", [128, G * 8], BF16))
        BI = es.enter_context(nc.sbuf_tensor("BI", [8, G * 128], BF16))
        AIf = es.enter_context(nc.sbuf_tensor("AIf", [128, G * 8], F32))
        lng = es.enter_context(nc.sbuf_tensor("lng", [128, G], F32))
        lnb = es.enter_context(nc.sbuf_tensor("lnb", [128, G], F32))
        HL = [es.enter_context(nc.sbuf_tensor(f"HL{g}", [128, SLAB], F32)) for g in range(G)]
        hb = [es.enter_context(nc.sbuf_tensor(f"hb{g}", [128, SLAB], BF16)) for g in range(G)]
        Kf = HL
        Kb = [es.enter_context(nc.sbuf_tensor(f"Kb{g}", [128, SLAB], BF16)) for g in range(G)]
        ka8 = es.enter_context(nc.sbuf_tensor("ka8", [8, SLAB], BF16))
        rot = [es.enter_context(nc.sbuf_tensor(f"rot{i}", [128, SLAB], F32)) for i in range(3)]
        onesf = es.enter_context(nc.sbuf_tensor("onesf", [128, SLAB], F32))
        carry = es.enter_context(nc.sbuf_tensor("carry", [128, G], F32))
        gsm = es.enter_context(nc.sbuf_tensor("gsm", [128, G], F32))

        p.op("vector", "memset", b["onesb"][:], 1.0)
        p.op("vector", "memset", b["epsc"][:], LN_EPS)
        p.op("vector", "memset", onesf[:], 1.0)
        p.op("vector", "memset", carry[:], 0.0)
        xld = emit_load_x(p, b, xin)
        p.dma("sync", lng[:], ln_g[:, :], "dsm")
        p.dma("sync", lnb[:], ln_b[:, :], "dsm")
        p.dma("sync", AIf[:], a_i.rearrange("p a c -> p (a c)"), "dsm")
        wl = WeightLoader(nc, es, p, "w")
        p.dma("sync", wl.stg[0][0:8, :], b_i.rearrange("p a c -> p (a c)")[:, 0:512], "dsm")
        p.dma("sync", wl.stg[1][0:8, :], b_i.rearrange("p a c -> p (a c)")[:, 512:1024], "dsm")
        sm_ld = p.cur("dsm")
        p.wait("vector", "dsm", sm_ld)
        p.op("vector", "tensor_copy", AI[:], AIf[:])
        p.op("vector", "tensor_copy", BI[0:8, 0:512], wl.stg[0][0:8, :])
        vbi = p.op("vector", "tensor_copy", BI[0:8, 512:1024], wl.stg[1][0:8, :])
        p.wait("sync", "v", vbi)
        for W_ap, Wb in ((w_q, WQ), (w_k, WK), (w_v, WV), (w_0, W0)):
            wdone = wl.load(W_ap.rearrange("p a b c -> p (a b c)"), Wb, 0, G * G * 128,
                            negate=(Wb is WV))
        p.wait("vector", "dx", xld)

        banks = Banks(p, b["psum"], range(2, 8))
        st_tgts = []

        def rot_take(idx):
            i3 = len(st_tgts) % 3
            if len(st_tgts) >= 3:
                p.wait("vector", f"dso{i3}", st_tgts[-3])
            return rot[i3]

        for s in range(NSLAB):
            ts = slice(s * SLAB, (s + 1) * SLAB)
            emit_ln_slab(p, b, s, out_fp32=HL)
            gam = [lng[:, g:g + 1] for g in range(G)]
            bet = [lnb[:, g:g + 1] for g in range(G)]
            lnv = emit_ln_slab(p, b, s, out_bf16=hb, src=HL, gamma=gam, beta=bet,
                               src_is_slab=True)

            def evq(dg, bi, bank, tmm, s=s, ts=ts):
                p.wait("vector", "t", tmm)
                rt = rot_take(len(st_tgts))
                ev = p.op("vector", "tensor_copy", rt[:], bank[:])
                banks.set_guard(bi, "v", ev)
                p.wait("gpsimd", "v", ev)
                st_tgts.append(p.dma("gpsimd", q_st[dg][:, ts], rt[:],
                                     f"dso{len(st_tgts) % 3}"))
            emit_proj(p, banks, WQ, hb, evq, wait_v=lnv)

            def evk(dg, bi, bank, tmm):
                p.wait("scalar", "t", tmm)
                ev1 = p.op("scalar", "copy", Kf[dg][:], bank[:])
                banks.set_guard(bi, "s", ev1)
                p.wait("vector", "s", ev1)
                p.op("vector", "tensor_copy", Kb[dg][:], Kf[dg][:])
            emit_proj(p, banks, WK, hb, evk)

            vkb = p.cur("v")
            p.wait("tensor", "v", vkb)
            bi8, bank8 = banks.take("tensor")
            for k in range(G):
                p.op("tensor", "matmul", bank8[0:8, :], AI[:, k * 8:(k + 1) * 8],
                     Kb[k][:], start=(k == 0), stop=(k == G - 1),
                     sem=("t" if k == G - 1 else None))
            t8 = p.cur("t")
            p.wait("vector", "t", t8)
            ev8 = p.op("vector", "tensor_copy", ka8[:], bank8[0:8, :])
            banks.set_guard(bi8, "v", ev8)

            p.wait("tensor", "v", ev8)
            for dg in range(G):
                bi, bank = banks.take("tensor")
                for k in range(G):
                    p.op("tensor", "matmul", bank[:],
                         W0[:, (dg * G + k) * 128:(dg * G + k + 1) * 128],
                         Kb[k][:], start=(k == 0), stop=False)
                p.op("tensor", "matmul", bank[:], BI[0:8, dg * 128:(dg + 1) * 128],
                     ka8[:], start=False, stop=True, sem="t")
                tmm = p.cur("t")
                p.wait("vector", "t", tmm)
                rt = rot_take(len(st_tgts))
                evy = p.op("vector", "tensor_copy", rt[:], bank[:])
                p.wait("tensor", "v", evy)
                for k in range(G):
                    p.op("tensor", "matmul", bank[:],
                         WV[:, (dg * G + k) * 128:(dg * G + k + 1) * 128],
                         hb[k][:], start=(k == 0), stop=(k == G - 1),
                         sem=("t" if k == G - 1 else None))
                terr = p.cur("t")
                p.wait("vector", "t", terr)
                p.op("vector", "tensor_tensor", b["Hs"][1][:], rt[:], bank[:], ALU.add)
                evg = p.op("vector", "tensor_tensor", b["Hs"][2][:], Kf[dg][:],
                           b["Hs"][1][:], ALU.mult)
                banks.set_guard(bi, "v", evg)
                p.wait("gpsimd", "v", evg)
                st_tgts.append(p.dma("gpsimd", y0_st[dg][:, ts], rt[:],
                                     f"dso{len(st_tgts) % 3}"))
                rt2 = rot_take(len(st_tgts))
                p.op("vector", "tensor_tensor_scan", rt2[:], onesf[:], b["Hs"][2][:],
                     carry[:, dg:dg + 1], ALU.mult, ALU.add)
                p.op("vector", "tensor_copy", carry[:, dg:dg + 1], rt2[:, SLAB - 1:SLAB])
                evc = p.cur("v")
                p.wait("gpsimd", "v", evc)
                st_tgts.append(p.dma("gpsimd", cum_st[dg][:, ts], rt2[:],
                                     f"dso{len(st_tgts) % 3}"))
        p.op("vector", "tensor_copy", gsm[:], carry[:])
        vg = p.cur("v")
        p.wait("gpsimd", "v", vg)
        p.dma("gpsimd", gsum[:, :], gsm[:], "dso0")
        p.wait_all("gpsimd", "dso", 3)
        p.build(es)
    return nc



def build_l0a2():
    nc = bass.Bass("TRN2", target_bir_lowering=False)
    xin = nc.dram_tensor("xin", [G, 128, T], F32, kind="ExternalInput").ap()
    q_st = nc.dram_tensor("q_st", [G, 128, T], F32, kind="ExternalInput").ap()
    y0_st = nc.dram_tensor("y0_st", [G, 128, T], F32, kind="ExternalInput").ap()
    cum_st = nc.dram_tensor("cum_st", [G, 128, T], F32, kind="ExternalInput").ap()
    gprev = nc.dram_tensor("gprev", [128, G], F32, kind="ExternalInput").ap()
    w_g = nc.dram_tensor("w_g", [128, G, G, 128], F32, kind="ExternalInput").ap()
    w_o = nc.dram_tensor("w_o", [128, G, G, 128], F32, kind="ExternalInput").ap()
    log_lr = nc.dram_tensor("log_lr", [128, G], F32, kind="ExternalInput").ap()
    ls_g = nc.dram_tensor("ls_g", [128, G], F32, kind="ExternalInput").ap()
    posb = nc.dram_tensor("posb", [128, T], F32, kind="ExternalInput").ap()
    xout = nc.dram_tensor("xout", [G, 128, T], F32, kind="ExternalOutput").ap()

    es = ExitStack()
    with es:
        p = Prog(nc)
        b = alloc_common(nc, es)
        WG = es.enter_context(nc.sbuf_tensor("WG", [128, G * G * 128], BF16))
        WO = es.enter_context(nc.sbuf_tensor("WO", [128, G * G * 128], BF16))
        lrv = es.enter_context(nc.sbuf_tensor("lrv", [128, G], F32))
        lr2 = es.enter_context(nc.sbuf_tensor("lr2", [128, G], F32))
        lsg = es.enter_context(nc.sbuf_tensor("lsg", [128, G], F32))
        gpv = es.enter_context(nc.sbuf_tensor("gpv", [128, G], F32))
        pos = es.enter_context(nc.sbuf_tensor("pos", [128, T], F32))
        HL = [es.enter_context(nc.sbuf_tensor(f"HL{g}", [128, SLAB], F32)) for g in range(G)]
        Hb = [es.enter_context(nc.sbuf_tensor(f"Hb{g}", [128, SLAB], BF16)) for g in range(G)]
        gate = [es.enter_context(nc.sbuf_tensor(f"gate{g}", [128, SLAB], BF16)) for g in range(G)]
        ob = [es.enter_context(nc.sbuf_tensor(f"ob{g}", [128, SLAB], BF16)) for g in range(G)]
        qrot = [es.enter_context(nc.sbuf_tensor(f"qrot{i}", [128, SLAB], F32)) for i in range(2)]
        yrot = [es.enter_context(nc.sbuf_tensor(f"yrot{i}", [128, SLAB], F32)) for i in range(2)]
        crot = [es.enter_context(nc.sbuf_tensor(f"crot{i}", [128, SLAB + 1], F32)) for i in range(2)]
        sc1 = es.enter_context(nc.sbuf_tensor("sc1", [128, SLAB], F32))
        sc2 = es.enter_context(nc.sbuf_tensor("sc2", [128, SLAB], F32))
        Xn = [es.enter_context(nc.sbuf_tensor(f"Xn{i}", [128, SLAB], F32)) for i in range(2)]

        p.op("vector", "memset", b["onesb"][:], 1.0)
        p.op("vector", "memset", b["epsc"][:], LN_EPS)
        xld = emit_load_x(p, b, xin)
        p.dma("sync", lrv[:], log_lr[:, :], "dsm")
        p.dma("sync", lsg[:], ls_g[:, :], "dsm")
        p.dma("sync", gpv[:], gprev[:, :], "dsm")
        p.dma("sync", pos[:], posb[:, :], "dsm")
        sm_ld = p.cur("dsm")
        p.wait("scalar", "dsm", sm_ld)
        se = p.op("scalar", "activation", lr2[:], lrv[:], AF.Exp)
        p.wait("vector", "s", se)
        p.op("vector", "tensor_scalar_min", lrv[:], lr2[:], 1.0)
        wl = WeightLoader(nc, es, p, "w")
        for W_ap, Wb in ((w_g, WG), (w_o, WO)):
            wdone = wl.load(W_ap.rearrange("p a b c -> p (a b c)"), Wb, 0, G * G * 128)
        p.wait("vector", "dx", xld)

        banks = Banks(p, b["psum"], range(2, 8))
        xo = []
        ld_tgts = []
        for s in range(NSLAB):
            ts = slice(s * SLAB, (s + 1) * SLAB)
            emit_ln_slab(p, b, s, out_fp32=HL, out_bf16=Hb)
            lnv = p.cur("v")

            def evg(dg, bi, bank, tmm):
                p.wait("scalar", "t", tmm)
                ev = p.op("scalar", "activation", gate[dg][:], bank[:], AF.Sigmoid)
                banks.set_guard(bi, "s", ev)
            emit_proj(p, banks, WG, Hb, evg, wait_v=lnv)
            sgate = p.cur("s")
            p.wait("vector", "s", sgate)

            for dg in range(G):
                par = dg % 2
                if len(ld_tgts) >= 2:
                    p.wait("sync", "v", ld_tgts[-2][1])
                t_q = p.dma("sync", qrot[par][:], q_st[dg][:, ts], "dld")
                t_y = p.dma("sync", yrot[par][:], y0_st[dg][:, ts], "dld")
                if s == 0:
                    p.dma("sync", crot[par][:, 1:SLAB + 1], cum_st[dg][:, 0:SLAB], "dld")
                else:
                    p.dma("sync", crot[par][:, 0:SLAB + 1],
                          cum_st[dg][:, s * SLAB - 1:(s + 1) * SLAB], "dld")
                t_c = p.cur("dld")
                p.wait("vector", "dld", t_c)
                if s == 0:
                    p.op("vector", "memset", crot[par][:, 0:1], 0.0)
                p.op("vector", "tensor_scalar", sc1[:], crot[par][:, 0:SLAB],
                     gpv[:, dg:dg + 1], None, ALU.add)
                p.op("vector", "tensor_tensor", sc2[:], qrot[par][:], sc1[:], ALU.mult)
                p.op("vector", "tensor_scalar", sc1[:], sc2[:], lrv[:, dg:dg + 1], None, ALU.mult)
                p.op("vector", "tensor_tensor", sc2[:], sc1[:], pos[:, ts], ALU.mult)
                p.op("vector", "tensor_tensor", sc1[:], yrot[par][:], sc2[:], ALU.subtract)
                lv = p.op("vector", "tensor_tensor", ob[dg][:], sc1[:], gate[dg][:], ALU.mult)
                ld_tgts.append((dg, lv))
            vob = p.cur("v")

            def evo(dg, bi, bank, tmm, ts=ts):
                p.wait("vector", "t", tmm)
                xp = len(xo) % 2
                if len(xo) >= 2:
                    p.wait("vector", f"dso{xp}", xo[-2])
                p.op("vector", "scalar_tensor_tensor", sc2[:], bank[:], lsg[:, dg:dg + 1],
                     HL[dg][:], ALU.mult, ALU.add)
                ev = p.op("vector", "tensor_tensor", Xn[xp][:], sc2[:], b["X"][dg][:, ts],
                          ALU.add)
                banks.set_guard(bi, "v", ev)
                p.wait("gpsimd", "v", ev)
                xo.append(p.dma("gpsimd", xout[dg][:, ts], Xn[xp][:], f"dso{xp}"))
            emit_proj(p, banks, WO, ob, evo, wait_v=vob)
        p.wait_all("gpsimd", "dso", 2)
        p.build(es)
    return nc


def pack_rhs(M):
    cin, cout = M.shape
    KC = cin // 128
    return np.ascontiguousarray(np.asarray(M).reshape(KC, 128, cout)
                                .transpose(1, 0, 2)).astype(np.float32)



def build_l1a():
    nc = bass.Bass("TRN2", target_bir_lowering=False)
    xin = nc.dram_tensor("xin", [G, 128, T], F32, kind="ExternalInput").ap()
    xeL = nc.dram_tensor("xeL", [G, 128, 128], F32, kind="ExternalInput").ap()
    xeR = nc.dram_tensor("xeR", [G, 128, 128], F32, kind="ExternalInput").ap()
    mskL = nc.dram_tensor("mskL", [128, 1], F32, kind="ExternalInput").ap()
    mskR = nc.dram_tensor("mskR", [128, 1], F32, kind="ExternalInput").ap()
    w_q = nc.dram_tensor("w_q", [128, G, G, 128], F32, kind="ExternalInput").ap()
    w_k = nc.dram_tensor("w_k", [128, G, G, 128], F32, kind="ExternalInput").ap()
    w_vt = nc.dram_tensor("w_vt", [128, G, C], F32, kind="ExternalInput").ap()
    w_o = nc.dram_tensor("w_o", [128, G, G, 128], F32, kind="ExternalInput").ap()
    b_q = nc.dram_tensor("b_q", [128, G], F32, kind="ExternalInput").ap()
    b_k = nc.dram_tensor("b_k", [128, G], F32, kind="ExternalInput").ap()
    b_vt = nc.dram_tensor("b_vt", [128, C], F32, kind="ExternalInput").ap()
    b_o = nc.dram_tensor("b_o", [128, G], F32, kind="ExternalInput").ap()
    tri1_h = nc.dram_tensor("tri1", [128, 128], F32, kind="ExternalInput").ap()
    tri2_h = nc.dram_tensor("tri2", [128, 128], F32, kind="ExternalInput").ap()
    xout = nc.dram_tensor("xout", [G, 128, T], F32, kind="ExternalOutput").ap()

    es = ExitStack()
    with es:
        p = Prog(nc)
        b = alloc_common(nc, es, want_X=False, n_hs=3)
        WQ = es.enter_context(nc.sbuf_tensor("WQ", [128, G * G * 128], BF16))
        WK = es.enter_context(nc.sbuf_tensor("WK", [128, G * G * 128], BF16))
        WVT = es.enter_context(nc.sbuf_tensor("WVT", [128, G * C], BF16))
        WO = es.enter_context(nc.sbuf_tensor("WO", [128, G * G * 128], BF16))
        bq = es.enter_context(nc.sbuf_tensor("bq", [128, G], F32))
        bk = es.enter_context(nc.sbuf_tensor("bk", [128, G], F32))
        bvb = es.enter_context(nc.sbuf_tensor("bvb", [128, C], F32))
        bo = es.enter_context(nc.sbuf_tensor("bo", [128, G], F32))
        ml = es.enter_context(nc.sbuf_tensor("ml", [128, 1], F32))
        mr = es.enter_context(nc.sbuf_tensor("mr", [128, 1], F32))
        tri1 = es.enter_context(nc.sbuf_tensor("tri1s", [128, 128], BF16))
        tri2 = es.enter_context(nc.sbuf_tensor("tri2s", [128, 128], BF16))
        Xs = [es.enter_context(nc.sbuf_tensor(f"Xs{g}", [128, SLAB], F32)) for g in range(G)]
        xrot = [es.enter_context(nc.sbuf_tensor(f"xrot{i}", [128, SLAB], F32)) for i in range(2)]
        XE = [es.enter_context(nc.sbuf_tensor(f"XE{g}", [128, 128], F32)) for g in range(G)]
        Hb = [[es.enter_context(nc.sbuf_tensor(f"Hb{i}{g}", [128, SLAB], BF16))
               for g in range(G)] for i in range(2)]
        HbE = [es.enter_context(nc.sbuf_tensor(f"HbE{g}", [128, 256], BF16)) for g in range(G)]
        Qb = [[es.enter_context(nc.sbuf_tensor(f"Qb{i}{g}", [128, SLAB], BF16))
               for g in range(G)] for i in range(2)]
        Kb = [[es.enter_context(nc.sbuf_tensor(f"Kb{i}{g}", [128, SLAB], BF16))
               for g in range(G)] for i in range(2)]
        KP = [es.enter_context(nc.sbuf_tensor(f"KP{g}", [128, 128], BF16)) for g in range(G)]
        KE = [es.enter_context(nc.sbuf_tensor(f"KE{g}", [128, 256], BF16)) for g in range(G)]
        Vt = [es.enter_context(nc.sbuf_tensor(f"Vt{i}", [128, C], BF16)) for i in range(8)]
        VP = es.enter_context(nc.sbuf_tensor("VP", [128, C], BF16))
        VE = [es.enter_context(nc.sbuf_tensor(f"VE{i}", [128, C], BF16)) for i in range(2)]
        expT = es.enter_context(nc.sbuf_tensor("expT", [128, 384], BF16))
        expM = es.enter_context(nc.sbuf_tensor("expM", [128, 384], BF16))
        rden = [es.enter_context(nc.sbuf_tensor(f"rden{i}", [128, 128], F32)) for i in range(2)]
        ob = [es.enter_context(nc.sbuf_tensor(f"ob{g}", [128, SLAB], BF16)) for g in range(G)]
        Xn = [es.enter_context(nc.sbuf_tensor(f"Xn{i}", [128, SLAB], F32)) for i in range(2)]

        p.op("vector", "memset", b["onesb"][:], 1.0)
        p.op("vector", "memset", b["epsc"][:], LN_EPS)
        p.dma("sync", bq[:], b_q[:, :], "dsm")
        p.dma("sync", bk[:], b_k[:, :], "dsm")
        p.dma("sync", bvb[:], b_vt[:, :], "dsm")
        p.dma("sync", bo[:], b_o[:, :], "dsm")
        p.dma("sync", ml[:], mskL[:, :], "dsm")
        p.dma("sync", mr[:], mskR[:, :], "dsm")
        trif = b["stat"][0]
        p.dma("sync", trif[:, 0:128], tri1_h[:, :], "dsm")
        sm1 = p.cur("dsm")
        p.wait("vector", "dsm", sm1)
        vtr = p.op("vector", "tensor_copy", tri1[:], trif[:, 0:128])
        p.wait("sync", "v", vtr)
        p.dma("sync", trif[:, 0:128], tri2_h[:, :], "dsm2")
        p.wait("vector", "dsm2", p.cur("dsm2"))
        p.op("vector", "tensor_copy", tri2[:], trif[:, 0:128])
        for g in range(G):
            p.dma("sync", XE[g][:, 0:128], xeL[g], "dxe")
        xel = p.cur("dxe")
        wl = WeightLoader(nc, es, p, "w")
        for W_ap, Wb, ncol in ((w_q, WQ, G * G * 128), (w_k, WK, G * G * 128),
                               (w_vt, WVT, G * C), (w_o, WO, G * G * 128)):
            wdone = wl.load(W_ap.rearrange("p a c -> p (a c)") if W_ap is w_vt
                            else W_ap.rearrange("p a b c -> p (a b c)"), Wb, 0, ncol)

        banks = Banks(p, b["psum"], range(2, 8))

        def proj_fm(W, rhs_tiles, rhs_c0, rhs_w, out_tiles, out_c0, bias, mask=None, wv=None):
            if wv is not None:
                p.wait("tensor", "v", wv)
            for dg in range(G):
                bi, bank = banks.take("tensor")
                for k in range(G):
                    p.op("tensor", "matmul", bank[:, 0:rhs_w],
                         W[:, (dg * G + k) * 128:(dg * G + k + 1) * 128],
                         rhs_tiles[k][:, rhs_c0:rhs_c0 + rhs_w], start=(k == 0),
                         stop=(k == G - 1), sem=("t" if k == G - 1 else None))
                tmm = p.cur("t")
                p.wait("vector", "t", tmm)
                if mask is None:
                    ev = p.op("vector", "tensor_scalar", out_tiles[dg][:, out_c0:out_c0 + rhs_w],
                              bank[:, 0:rhs_w], bias[:, dg:dg + 1], None, ALU.add)
                else:
                    ev = p.op("vector", "tensor_scalar", out_tiles[dg][:, out_c0:out_c0 + rhs_w],
                              bank[:, 0:rhs_w], bias[:, dg:dg + 1], mask, ALU.add, ALU.mult)
                banks.set_guard(bi, "v", ev)

        def proj_tok(lhs_tiles, lhs_c0, n_tok, vt_dst, mask=None, wv=None):
            if wv is not None:
                p.wait("tensor", "v", wv)
            for half in range(2):
                bi, bank = banks.take("tensor")
                for k in range(G):
                    p.op("tensor", "matmul", bank[0:n_tok, :],
                         lhs_tiles[k][:, lhs_c0:lhs_c0 + n_tok],
                         WVT[:, k * C + half * 512:k * C + half * 512 + 512],
                         start=(k == 0), stop=(k == G - 1),
                         sem=("t" if k == G - 1 else None))
                tmm = p.cur("t")
                p.wait("vector", "t", tmm)
                if mask is None:
                    ev = p.op("vector", "tensor_tensor",
                              vt_dst[0:n_tok, half * 512:half * 512 + 512],
                              bank[0:n_tok, :], bvb[0:n_tok, half * 512:half * 512 + 512],
                              ALU.add)
                else:
                    p.op("vector", "tensor_tensor", b["Hs"][1][0:n_tok, :], bank[0:n_tok, :],
                         bvb[0:n_tok, half * 512:half * 512 + 512], ALU.add)
                    ev = p.op("vector", "tensor_scalar",
                              vt_dst[0:n_tok, half * 512:half * 512 + 512],
                              b["Hs"][1][0:n_tok, :], mask, None, ALU.mult)
                banks.set_guard(bi, "v", ev)

        p.wait("vector", "dxe", xel)
        emit_ln_slab(p, b, 0, out_bf16=HbE, src=XE, src_is_slab=True, width=128)
        veL = p.cur("v")
        p.wait("sync", "v", veL)
        for g in range(G):
            p.dma("sync", XE[g][:, 0:128], xeR[g], "dxe")
        p.wait("vector", "dxe", p.cur("dxe"))
        emit_ln_slab(p, b, 0, out_bf16=HbE, src=XE, src_is_slab=True, width=128,
                     out_col0=128)
        ve = p.cur("v")
        proj_fm(WK, HbE, 0, 128, KE, 0, bk, mask=ml[:, 0:1], wv=ve)
        proj_fm(WK, HbE, 128, 128, KE, 128, bk, mask=mr[:, 0:1])
        proj_tok(HbE, 0, 128, VE[0], mask=ml[:, 0:1])
        proj_tok(HbE, 128, 128, VE[1], mask=mr[:, 0:1])

        for g in range(G):
            p.dma("sync", Xs[g][:], xin[g][:, 0:SLAB], "dxl")
        p.wait("vector", "dxl", p.cur("dxl"))
        emit_ln_slab(p, b, 0, out_bf16=Hb[0], src=Xs, src_is_slab=True)
        v0 = p.cur("v")
        proj_fm(WQ, Hb[0], 0, SLAB, Qb[0], 0, bq, wv=v0)
        proj_fm(WK, Hb[0], 0, SLAB, Kb[0], 0, bk)
        for tt in range(4):
            proj_tok(Hb[0], tt * 128, 128, Vt[tt])

        KPREV = [-99]

        def v_tile(kt):
            if kt == -1:
                return VE[0]
            if kt == 16:
                return VE[1]
            if kt == KPREV[0]:
                return VP
            return Vt[((kt // 4) % 2) * 4 + kt % 4]

        def k_tile_ap(g, kt):
            if kt == -1:
                return KE[g][:, 0:128]
            if kt == 16:
                return KE[g][:, 128:256]
            if kt == KPREV[0]:
                return KP[g][:, 0:128]
            gen = (kt // 4) % 2
            j = kt % 4
            return Kb[gen][g][:, j * 128:(j + 1) * 128]

        xo = []
        vln = [p.cur("v")]
        for s in range(NSLAB):
            if s + 1 < NSLAB:
                gen = (s + 1) % 2
                p.wait("sync", "v", vln[0])
                for g in range(G):
                    p.dma("sync", Xs[g][:], xin[g][:, (s + 1) * SLAB:(s + 2) * SLAB], "dxl")
                p.wait("vector", "dxl", p.cur("dxl"))
                emit_ln_slab(p, b, 0, out_bf16=Hb[gen], src=Xs, src_is_slab=True)
                vln[0] = p.cur("v")
                proj_fm(WQ, Hb[gen], 0, SLAB, Qb[gen], 0, bq, wv=vln[0])
                proj_fm(WK, Hb[gen], 0, SLAB, Kb[gen], 0, bk)
                for tt in range(4):
                    proj_tok(Hb[gen], tt * 128, 128, Vt[gen * 4 + tt])
            sgen = s % 2
            vq = p.cur("v")
            p.wait("tensor", "v", vq)
            for j in range(4):
                tqg = s * 4 + j
                kts = [tqg - 1, tqg, tqg + 1]
                for h in range(16):
                    hg, ho = h // 2, 64 * (h % 2)
                    qap = Qb[sgen][hg][ho:ho + 64, j * 128:(j + 1) * 128]
                    bi_s, bank_s = banks.take("tensor")
                    for ki, kt in enumerate(kts):
                        kap = k_tile_ap(hg, kt)[ho:ho + 64, :]
                        p.op("tensor", "matmul", bank_s[:, ki * 128:(ki + 1) * 128],
                             kap, qap, start=True, stop=True,
                             sem=("t" if ki == 2 else None))
                    tsc = p.cur("t")
                    p.wait("scalar", "t", tsc)
                    se = p.op("scalar", "activation", expT[:], bank_s[:, 0:384], AF.Exp,
                              scale=0.125)
                    banks.set_guard(bi_s, "s", se)
                    p.wait("vector", "s", se)
                    p.op("vector", "tensor_tensor", expM[:, 0:128], expT[:, 0:128],
                         tri1[:], ALU.mult)
                    p.op("vector", "tensor_copy", expM[:, 128:256], expT[:, 128:256])
                    vm = p.op("vector", "tensor_tensor", expM[:, 256:384], expT[:, 256:384],
                              tri2[:], ALU.mult)
                    bi_d, bank_d = banks.take("tensor")
                    p.wait("tensor", "v", vm)
                    for ki in range(3):
                        p.op("tensor", "matmul", bank_d[:, 0:128], b["onesb"][:],
                             expM[:, ki * 128:(ki + 1) * 128], start=(ki == 0),
                             stop=(ki == 2), sem=("t" if ki == 2 else None))
                    tden = p.cur("t")
                    p.wait("scalar", "t", tden)
                    p.op("scalar", "activation", b["stat"][0][:, 0:128], bank_d[:, 0:128],
                         AF.Ln)
                    sd = p.op("scalar", "activation", rden[h % 2][:], b["stat"][0][:, 0:128],
                              AF.Exp, scale=-1.0)
                    banks.set_guard(bi_d, "s", sd)
                    if h % 2 == 0:
                        bi_o, bank_o = banks.take("tensor")
                    for ki, kt in enumerate(kts):
                        p.op("tensor", "matmul", bank_o[ho:ho + 64, 0:128],
                             v_tile(kt)[:, h * 64:(h + 1) * 64],
                             expM[:, ki * 128:(ki + 1) * 128], start=(ki == 0),
                             stop=(ki == 2), sem=("t" if ki == 2 else None))
                    tpv = p.cur("t")
                    if h % 2 == 1:
                        p.wait("vector", "t", tpv)
                        p.wait("vector", "s", sd)
                        for hh, par in ((h - 1, 0), (h, 1)):
                            hof = 64 * (hh % 2)
                            evo = p.op("vector", "tensor_tensor",
                                       ob[hh // 2][hof:hof + 64, j * 128:(j + 1) * 128],
                                       bank_o[hof:hof + 64, 0:128],
                                       rden[par][hof:hof + 64, :], ALU.mult)
                        banks.set_guard(bi_o, "v", evo)
            vob = p.cur("v")
            p.wait("tensor", "v", vob)
            xres_v = [0, 0]
            for dg in range(G):
                xp = dg % 2
                if dg >= 2:
                    p.wait("scalar", "v", xres_v[xp])
                p.dma("scalar", xrot[xp][:], xin[dg][:, s * SLAB:(s + 1) * SLAB], "dxr")
                bi, bank = banks.take("tensor")
                for k in range(G):
                    p.op("tensor", "matmul", bank[:],
                         WO[:, (dg * G + k) * 128:(dg * G + k + 1) * 128],
                         ob[k][:], start=(k == 0), stop=(k == G - 1),
                         sem=("t" if k == G - 1 else None))
                tmm = p.cur("t")
                p.wait("vector", "t", tmm)
                p.wait("vector", "dxr", p.cur("dxr"))
                if len(xo) >= 2:
                    p.wait("vector", f"dso{len(xo) % 2}", xo[-2])
                xnp = len(xo) % 2
                ev = p.op("vector", "scalar_tensor_tensor", Xn[xnp][:], bank[:],
                          bo[:, dg:dg + 1], xrot[xp][:], ALU.add, ALU.add)
                xres_v[xp] = ev
                banks.set_guard(bi, "v", ev)
                p.wait("gpsimd", "v", ev)
                xo.append(p.dma("gpsimd", xout[dg][:, s * SLAB:(s + 1) * SLAB],
                                Xn[xnp][:], f"dso{xnp}"))
            if s + 1 < NSLAB:
                for g in range(G):
                    p.op("vector", "tensor_copy", KP[g][:], Kb[sgen][g][:, 384:512])
                p.op("vector", "tensor_copy", VP[:], Vt[sgen * 4 + 3][:])
                KPREV[0] = s * 4 + 3
        p.wait_all("gpsimd", "dso", 2)
        p.build(es)
    return nc



def build_l2f():
    nc = bass.Bass("TRN2", target_bir_lowering=False)
    DGF = 32
    xin = nc.dram_tensor("xin", [G, 128, T], F32, kind="ExternalInput").ap()
    w_fc = nc.dram_tensor("w_fc", [128, DGF, G, 128], F32, kind="ExternalInput").ap()
    w_pr = nc.dram_tensor("w_pr", [128, G, DGF, 128], F32, kind="ExternalInput").ap()
    xout = nc.dram_tensor("xout", [G, 128, T], F32, kind="ExternalOutput").ap()

    es = ExitStack()
    with es:
        p = Prog(nc)
        b = alloc_common(nc, es, want_X=False, n_hs=3)
        WFC = es.enter_context(nc.sbuf_tensor("WFC", [128, DGF * G * 128], BF16))
        WPB = [es.enter_context(nc.sbuf_tensor(f"WPB{i}", [128, DGF * 128], BF16))
               for i in range(2)]
        Xs = [es.enter_context(nc.sbuf_tensor(f"Xs{g}", [128, SLAB], F32)) for g in range(G)]
        Hb = [es.enter_context(nc.sbuf_tensor(f"Hb{g}", [128, SLAB], BF16)) for g in range(G)]
        hA = [es.enter_context(nc.sbuf_tensor(f"hA{i}", [128, SLAB], BF16)) for i in range(DGF)]
        rk = es.enter_context(nc.sbuf_tensor("rk", [128, SLAB], F32))
        yv = [es.enter_context(nc.sbuf_tensor(f"yv{g}", [128, SLAB], F32)) for g in range(G)]
        mem = [es.enter_context(nc.sbuf_tensor(f"mem{i}", [128, SLAB], F32)) for i in range(2)]
        acc = [es.enter_context(nc.sbuf_tensor(f"acc{i}", [128, SLAB], F32)) for i in range(2)]
        cmp_ = es.enter_context(nc.sbuf_tensor("cmp", [128, SLAB], F32))
        uu = es.enter_context(nc.sbuf_tensor("uu", [128, SLAB], F32))
        Xn = [es.enter_context(nc.sbuf_tensor(f"Xn{i}", [128, SLAB], F32)) for i in range(2)]

        p.op("vector", "memset", b["onesb"][:], 1.0)
        p.op("vector", "memset", b["epsc"][:], LN_EPS)
        wl = WeightLoader(nc, es, p, "w")
        wdone = wl.load(w_fc.rearrange("p a b c -> p (a b c)"), WFC, 0, DGF * G * 128)

        banks = Banks(p, b["psum"], range(0, 8))
        xo = []
        vln = [0]
        wpb_guard = [0, 0]
        for s in range(NSLAB):
            ts = slice(s * SLAB, (s + 1) * SLAB)
            p.wait("sync", "v", vln[0])
            for g in range(G):
                p.dma("sync", Xs[g][:], xin[g][:, ts], "dxl")
            p.wait("vector", "dxl", p.cur("dxl"))
            emit_ln_slab(p, b, 0, out_bf16=Hb, src=Xs, src_is_slab=True)
            vln[0] = p.cur("v")
            if s == 0:
                p.wait("tensor", "v", wdone)
            p.wait("tensor", "v", vln[0])
            sv_rel = 0
            for kc in range(DGF):
                bi, bank = banks.take("tensor")
                for k in range(G):
                    p.op("tensor", "matmul", bank[:],
                         WFC[:, (kc * G + k) * 128:(kc * G + k + 1) * 128],
                         Hb[k][:], start=(k == 0), stop=(k == G - 1),
                         sem=("t" if k == G - 1 else None))
                tmm = p.cur("t")
                p.wait("scalar", "t", tmm)
                if sv_rel:
                    p.wait("scalar", "v", sv_rel)
                srel = p.op("scalar", "activation", rk[:], bank[:], AF.Relu)
                banks.set_guard(bi, "s", srel)
                p.wait("vector", "s", srel)
                sv_rel = p.op("vector", "tensor_tensor", hA[kc][:], rk[:], rk[:], ALU.mult)
            vha = p.cur("v")
            p.wait("tensor", "v", vha)
            for dg in range(G):
                par = dg % 2
                if wpb_guard[par]:
                    p.wait("sync", "t", wpb_guard[par])
                wld = wl.load(w_pr.rearrange("p a b c -> p (a b c)")
                              [:, dg * DGF * 128:(dg + 1) * DGF * 128],
                              WPB[par], 0, DGF * 128)
                bi, bank = banks.take("tensor")
                p.wait("tensor", "v", wld)
                for kc in range(DGF):
                    p.op("tensor", "matmul", bank[:],
                         WPB[par][:, kc * 128:(kc + 1) * 128], hA[kc][:],
                         start=(kc == 0), stop=(kc == DGF - 1),
                         sem=("t" if kc == DGF - 1 else None))
                tmm = p.cur("t")
                wpb_guard[par] = tmm
                p.wait("vector", "t", tmm)
                ev = p.op("vector", "tensor_copy", yv[dg][:], bank[:])
                banks.set_guard(bi, "v", ev)
                y = yv[dg]
                p.op("vector", "tensor_scalar", cmp_[:], y[:], 1.0, None, ALU.is_gt)
                p.op("vector", "tensor_copy", acc[0][:], cmp_[:])
                mprev, mcur, ai = y, 0, 0
                for k in range(4):
                    p.op("vector", "scalar_tensor_tensor", uu[:], mprev[:], 0.9, y[:],
                         ALU.mult, ALU.add)
                    mcur = mem[k % 2]
                    p.op("vector", "tensor_tensor", mcur[:], uu[:], cmp_[:], ALU.subtract)
                    p.op("vector", "tensor_scalar", cmp_[:], mcur[:], 1.0, None, ALU.is_gt)
                    p.op("vector", "tensor_tensor", acc[1 - ai][:], acc[ai][:], cmp_[:],
                         ALU.add)
                    ai = 1 - ai
                    mprev = mcur
                xnp = len(xo) % 2
                if len(xo) >= 2:
                    p.wait("vector", f"dso{xnp}", xo[-2])
                ev = p.op("vector", "scalar_tensor_tensor", Xn[xnp][:], acc[ai][:], 0.1,
                          Xs[dg][:], ALU.mult, ALU.add)
                p.wait("gpsimd", "v", ev)
                xo.append(p.dma("gpsimd", xout[dg][:, ts], Xn[xnp][:], f"dso{xnp}"))
            vln[0] = p.cur("v")
        p.wait_all("gpsimd", "dso", 2)
        p.build(es)
    return nc



def build_l3f():
    nc = bass.Bass("TRN2", target_bir_lowering=False)
    DGF = 32
    xin = nc.dram_tensor("xin", [G, 128, T], F32, kind="ExternalInput").ap()
    w_1 = nc.dram_tensor("w_1", [128, DGF, G, 128], F32, kind="ExternalInput").ap()
    w_2 = nc.dram_tensor("w_2", [128, DGF, G, 128], F32, kind="ExternalInput").ap()
    h_1 = nc.dram_tensor("h_1", [128, DGF, DGF, 128], F32, kind="ExternalInput").ap()
    h_2 = nc.dram_tensor("h_2", [128, DGF, DGF, 128], F32, kind="ExternalInput").ap()
    w_3 = nc.dram_tensor("w_3", [128, G, DGF, 128], F32, kind="ExternalInput").ap()
    s_1 = nc.dram_tensor("s_1", [128, DGF], F32, kind="ExternalInput").ap()
    s_2 = nc.dram_tensor("s_2", [128, DGF], F32, kind="ExternalInput").ap()
    s_h1 = nc.dram_tensor("s_h1", [128, DGF], F32, kind="ExternalInput").ap()
    s_h2 = nc.dram_tensor("s_h2", [128, DGF], F32, kind="ExternalInput").ap()
    s_3 = nc.dram_tensor("s_3", [128, G], F32, kind="ExternalInput").ap()
    xout = nc.dram_tensor("xout", [G, 128, T], F32, kind="ExternalOutput").ap()
    st_a = nc.dram_tensor("st_a", [DGF, 128, T], BF16)
    st_b = nc.dram_tensor("st_b", [DGF, 128, T], BF16)

    BLK = 8
    NBLK = DGF // BLK
    es = ExitStack()
    with es:
        p = Prog(nc)
        b = alloc_common(nc, es, want_X=False, n_hs=3)
        WH = es.enter_context(nc.sbuf_tensor("WH", [128, BLK * DGF * 128], BF16))
        WB1 = WH.ap()[:, 0:BLK * G * 128]
        WB2 = WH.ap()[:, BLK * G * 128:2 * BLK * G * 128]
        Xs = [es.enter_context(nc.sbuf_tensor(f"Xs{g}", [128, SLAB], F32)) for g in range(G)]
        Hb = [es.enter_context(nc.sbuf_tensor(f"HbF{g}", [128, T], BF16)) for g in range(G)]
        hsl = [[es.enter_context(nc.sbuf_tensor(f"hsl{j}_{i}", [128, SLAB], BF16))
                for i in range(DGF)] for j in range(2)]
        sc1 = es.enter_context(nc.sbuf_tensor("sc1", [128, DGF], F32))
        sc2 = es.enter_context(nc.sbuf_tensor("sc2", [128, DGF], F32))
        sch1 = es.enter_context(nc.sbuf_tensor("sch1", [128, DGF], F32))
        sch2 = es.enter_context(nc.sbuf_tensor("sch2", [128, DGF], F32))
        sc3 = es.enter_context(nc.sbuf_tensor("sc3", [128, G], F32))
        sc3h = es.enter_context(nc.sbuf_tensor("sc3h", [128, G], F32))
        arot = [b["stat"][1], b["stat"][2]]
        brot = [es.enter_context(nc.sbuf_tensor(f"brot{i}", [128, SLAB], BF16)) for i in range(4)]
        Xn = [es.enter_context(nc.sbuf_tensor(f"Xn{i}", [128, SLAB], F32)) for i in range(2)]

        p.op("vector", "memset", b["onesb"][:], 1.0)
        p.op("vector", "memset", b["epsc"][:], LN_EPS)
        for ap_, dst in ((s_1, sc1), (s_2, sc2), (s_h1, sch1), (s_h2, sch2), (s_3, sc3)):
            p.dma("sync", dst[:], ap_[:, :], "dsm")
        p.wait("vector", "dsm", p.cur("dsm"))
        p.op("vector", "tensor_scalar", sc3h[:], sc3[:], 0.5, None, ALU.mult)
        wl = WeightLoader(nc, es, p, "w")

        vln = 0
        for s in range(NSLAB):
            p.wait("sync", "v", vln)
            for g in range(G):
                p.dma("sync", Xs[g][:], xin[g][:, s * SLAB:(s + 1) * SLAB], "dxl")
            p.wait("vector", "dxl", p.cur("dxl"))
            vln = emit_ln_slab(p, b, 0, out_bf16=Hb, src=Xs, src_is_slab=True,
                               out_col0=s * SLAB)
        vhb = p.cur("v")

        banks = Banks(p, b["psum"], range(0, 8))
        st = []
        vsig = []

        w1f = w_1.rearrange("p a b c -> p (a b c)")
        w2f = w_2.rearrange("p a b c -> p (a b c)")
        wg1 = [0, 0]
        for blk in range(NBLK):
            if wg1[0]:
                p.wait("sync", "t", wg1[0])
            wd1 = wl.load(w1f[:, blk * BLK * G * 128:(blk + 1) * BLK * G * 128], WB1, 0,
                          BLK * G * 128)
            wd2 = wl.load(w2f[:, blk * BLK * G * 128:(blk + 1) * BLK * G * 128], WB2, 0,
                          BLK * G * 128)
            p.wait("tensor", "v", max(wd1, wd2, vhb))
            for s in range(NSLAB):
                ts = slice(s * SLAB, (s + 1) * SLAB)
                for j in range(BLK):
                    dg = blk * BLK + j
                    bi, bank = banks.take("tensor")
                    for k in range(G):
                        p.op("tensor", "matmul", bank[:],
                             WB1[:, (j * G + k) * 128:(j * G + k + 1) * 128],
                             Hb[k][:, ts], start=(k == 0), stop=(k == G - 1),
                             sem=("t" if k == G - 1 else None))
                    t1 = p.cur("t")
                    bi2, bank2 = banks.take("tensor")
                    for k in range(G):
                        p.op("tensor", "matmul", bank2[:],
                             WB2[:, (j * G + k) * 128:(j * G + k + 1) * 128],
                             Hb[k][:, ts], start=(k == 0), stop=(k == G - 1),
                             sem=("t" if k == G - 1 else None))
                    t2 = p.cur("t")
                    wg1[0] = t2
                    ar = arot[len(st) % 2]
                    br = brot[len(st) % 4]
                    p.wait("scalar", "t", t1)
                    if len(st) >= 2:
                        p.wait("scalar", "v", vsig[len(st) - 2])
                    sev = p.op("scalar", "activation", ar[:], bank[:], AF.Silu,
                               scale=sc1[:, dg:dg + 1])
                    banks.set_guard(bi, "s", sev)
                    p.wait("vector", "s", sev)
                    p.wait("vector", "t", t2)
                    if len(st) >= 4:
                        p.wait("vector", f"dso{len(st) % 4}", st[-4])
                    vv = p.op("vector", "scalar_tensor_tensor", br[:], bank2[:],
                              sc2[:, dg:dg + 1], ar[:], ALU.mult, ALU.mult)
                    banks.set_guard(bi2, "v", vv)
                    vsig.append(vv)
                    p.wait("gpsimd", "v", vv)
                    st.append(p.dma("gpsimd", st_a.ap()[dg][:, ts], br[:],
                                    f"dso{len(st) % 4}"))

        def h_phase(win, s_col, src_st, dst_st):
            wf = win.rearrange("p a b c -> p (a b c)")
            hguard = [p.cur("t")]
            p.wait_all("sync", "dso", 4)
            iters = [(blk, s) for blk in range(NBLK) for s in range(NSLAB)]
            dhl_tgts = []
            mm_tgts = {}

            def issue_loads(idx):
                blk_, s_ = iters[idx]
                par_ = idx % 2
                ts_ = slice(s_ * SLAB, (s_ + 1) * SLAB)
                if idx >= 2:
                    p.wait("sync", "t", mm_tgts[idx - 2])
                for kc in range(DGF):
                    p.dma("sync", hsl[par_][kc][:], src_st.ap()[kc][:, ts_], f"dhl{par_}")
                dhl_tgts.append(p.cur(f"dhl{par_}"))

            issue_loads(0)
            for idx, (blk, s) in enumerate(iters):
                par = idx % 2
                ts = slice(s * SLAB, (s + 1) * SLAB)
                if s == 0:
                    if hguard[0]:
                        p.wait("sync", "t", hguard[0])
                    wd = wl.load(wf[:, blk * BLK * DGF * 128:(blk + 1) * BLK * DGF * 128],
                                 WH, 0, BLK * DGF * 128)
                    p.wait("tensor", "v", wd)
                if idx + 1 < len(iters):
                    issue_loads(idx + 1)
                p.wait("tensor", f"dhl{idx % 2}", dhl_tgts[idx])
                for j in range(BLK):
                    dg = blk * BLK + j
                    bi, bank = banks.take("tensor")
                    for kc in range(DGF):
                        p.op("tensor", "matmul", bank[:],
                             WH[:, (j * DGF + kc) * 128:(j * DGF + kc + 1) * 128],
                             hsl[par][kc][:], start=(kc == 0), stop=(kc == DGF - 1),
                             sem=("t" if kc == DGF - 1 else None))
                    tmm = p.cur("t")
                    hguard[0] = tmm
                    mm_tgts[idx] = tmm
                    p.wait("scalar", "t", tmm)
                    br = brot[len(st) % 4]
                    if len(st) >= 4:
                        p.wait("scalar", f"dso{len(st) % 4}", st[-4])
                    sev = p.op("scalar", "activation", br[:], bank[:], AF.Gelu,
                               scale=s_col[:, dg:dg + 1])
                    banks.set_guard(bi, "s", sev)
                    p.wait("gpsimd", "s", sev)
                    st.append(p.dma("gpsimd", dst_st.ap()[dg][:, ts], br[:],
                                    f"dso{len(st) % 4}"))
        h_phase(h_1, sch1, st_a, st_b)
        h_phase(h_2, sch2, st_b, st_a)

        w3f = w_3.rearrange("p a b c -> p (a b c)")
        p.wait("sync", "t", p.cur("t"))
        wd = wl.load(w3f, WH, 0, G * DGF * 128)
        p.wait("tensor", "v", wd)
        xo = []
        p.wait_all("sync", "dso", 4)
        p4mm = {}
        p4v = {}
        for s in range(NSLAB):
            ts = slice(s * SLAB, (s + 1) * SLAB)
            par = s % 2
            if s >= 2:
                p.wait("sync", "t", p4mm[s - 2])
            if s >= 1:
                p.wait("sync", "v", p4v[s - 1])
            for kc in range(DGF):
                p.dma("sync", hsl[par][kc][:], st_a.ap()[kc][:, ts], f"dhp{par}")
            for g in range(G):
                p.dma("sync", Xs[g][:], xin[g][:, ts], "dxl")
            p.wait("tensor", f"dhp{par}", p.cur(f"dhp{par}"))
            for dg in range(G):
                bi, bank = banks.take("tensor")
                for kc in range(DGF):
                    p.op("tensor", "matmul", bank[:],
                         WH[:, (dg * DGF + kc) * 128:(dg * DGF + kc + 1) * 128],
                         hsl[par][kc][:], start=(kc == 0), stop=(kc == DGF - 1),
                         sem=("t" if kc == DGF - 1 else None))
                p4mm[s] = p.cur("t")
                tmm = p.cur("t")
                p.wait("vector", "t", tmm)
                p.wait("vector", "dxl", p.cur("dxl"))
                xnp = len(xo) % 2
                if len(xo) >= 2:
                    p.wait("vector", f"dso2{xnp}", xo[-2])
                ev = p.op("vector", "scalar_tensor_tensor", Xn[xnp][:], bank[:],
                          sc3h[:, dg:dg + 1], Xs[dg][:], ALU.mult, ALU.add)
                banks.set_guard(bi, "v", ev)
                p.wait("gpsimd", "v", ev)
                xo.append(p.dma("gpsimd", xout[dg][:, ts], Xn[xnp][:], f"dso2{xnp}"))
            p4v[s] = p.cur("v")
        p.wait_all("gpsimd", "dso2", 2)
        p.build(es)
    return nc



def build_l3a1():
    nc = bass.Bass("TRN2", target_bir_lowering=False)
    E = C + 1
    xin = nc.dram_tensor("xin", [G, 128, T], F32, kind="ExternalInput").ap()
    w_q = nc.dram_tensor("w_q", [128, G, G, 128], F32, kind="ExternalInput").ap()
    w_k = nc.dram_tensor("w_k", [128, G, G, 128], F32, kind="ExternalInput").ap()
    w_vt = nc.dram_tensor("w_vt", [128, G, C], F32, kind="ExternalInput").ap()
    idm = nc.dram_tensor("idm", [128, 128], F32, kind="ExternalInput").ap()
    bcm = nc.dram_tensor("bcm", [2, 128, 256], F32, kind="ExternalInput").ap()
    s_out = nc.dram_tensor("s_out", [G, 128, E], F32, kind="ExternalOutput").ap()
    sq_st = nc.dram_tensor("sq_st", [G, 128, T], BF16, kind="ExternalOutput").ap()
    o_st = nc.dram_tensor("o_st", [G, 128, T], F32, kind="ExternalOutput").ap()
    den_st = nc.dram_tensor("den_st", [1, T], F32, kind="ExternalOutput").ap()

    CH3 = 256
    NCH = T // CH3
    NST = CH3 // 128
    es = ExitStack()
    with es:
        p = Prog(nc)
        b = alloc_common(nc, es, want_X=False, n_hs=3, n_psum=7)
        ps_t = es.enter_context(nc.psum_tensor("ps_t", [128, 1024], BF16))
        WQ = es.enter_context(nc.sbuf_tensor("WQ", [128, G * G * 128], BF16))
        WK = es.enter_context(nc.sbuf_tensor("WK", [128, G * G * 128], BF16))
        WVT = es.enter_context(nc.sbuf_tensor("WVT", [128, G * C], BF16))
        S = [[es.enter_context(nc.sbuf_tensor(f"S{i}{g}", [128, E], F32)) for g in range(G)]
             for i in range(2)]
        Sb = [es.enter_context(nc.sbuf_tensor(f"Sb{g}", [128, E], BF16)) for g in range(G)]
        Xs = [es.enter_context(nc.sbuf_tensor(f"Xs{g}", [128, CH3], F32)) for g in range(G)]
        Hb = [es.enter_context(nc.sbuf_tensor(f"Hb{g}", [128, CH3], BF16)) for g in range(G)]
        SQ = [es.enter_context(nc.sbuf_tensor(f"SQ{g}", [128, CH3], BF16)) for g in range(G)]
        SK = [es.enter_context(nc.sbuf_tensor(f"SK{g}", [128, CH3], BF16)) for g in range(G)]
        Vt = [es.enter_context(nc.sbuf_tensor(f"Vt{i}", [128, E], BF16)) for i in range(NST)]
        SKt = [es.enter_context(nc.sbuf_tensor(f"SKt{i}", [128, C], BF16)) for i in range(NST)]
        Ab = [es.enter_context(nc.sbuf_tensor(f"Ab{i}", [128, CH3], BF16)) for i in range(NST)]
        bcmk = [es.enter_context(nc.sbuf_tensor(f"bcmk{i}", [128, CH3], BF16)) for i in range(NST)]
        ident = es.enter_context(nc.sbuf_tensor("ident", [128, 128], BF16))
        el1 = b["Hs"][1]
        el2 = b["Hs"][2]
        orot = [es.enter_context(nc.sbuf_tensor(f"orot{i}", [128, CH3], F32)) for i in range(2)]
        denr = es.enter_context(nc.sbuf_tensor("denr", [1, CH3], F32))

        p.op("vector", "memset", b["onesb"][:], 1.0)
        p.op("vector", "memset", b["epsc"][:], LN_EPS)
        for g in range(G):
            p.op("vector", "memset", S[0][g][:], 0.0)
        for j in range(NST):
            p.op("vector", "memset", Vt[j][:, C:E], 1.0)
        p.dma("sync", b["stat"][0][:, 0:128], idm[:, :], "dsm")
        p.wait("vector", "dsm", p.cur("dsm"))
        vid = p.op("vector", "tensor_copy", ident[:], b["stat"][0][:, 0:128])
        p.wait("sync", "v", vid)
        for j in range(NST):
            p.dma("sync", b["stat"][0][:, 0:CH3], bcm[j], "dbc")
            p.wait("vector", "dbc", p.cur("dbc"))
            vbc = p.op("vector", "tensor_copy", bcmk[j][:], b["stat"][0][:, 0:CH3])
            p.wait("sync", "v", vbc)
        wl = WeightLoader(nc, es, p, "w")
        for W_ap, Wb, ncol in ((w_q, WQ, G * G * 128), (w_k, WK, G * G * 128),
                               (w_vt, WVT, G * C)):
            wdone = wl.load(W_ap.rearrange("p a c -> p (a c)") if W_ap is w_vt
                            else W_ap.rearrange("p a b c -> p (a b c)"), Wb, 0, ncol)

        banks = Banks(p, b["psum"], range(2, 7))
        vln = [0]
        st = []
        odma = [0, 0]
        denw = [0]

        def elu_evac(bank, out_b, stash_ap=None):
            p.op("vector", "tensor_scalar_min", el1[:, 0:CH3], bank[:, 0:CH3], 0.0)
            vmin = p.op("vector", "tensor_scalar_max", el2[:, 0:CH3], bank[:, 0:CH3], 0.0)
            p.wait("scalar", "v", vmin)
            sex = p.op("scalar", "activation", el2[:, CH3:2 * CH3], el1[:, 0:CH3], AF.Exp)
            p.wait("vector", "s", sex)
            ev = p.op("vector", "tensor_tensor", out_b[:], el2[:, CH3:2 * CH3],
                      el2[:, 0:CH3], ALU.add)
            if stash_ap is not None:
                p.wait("gpsimd", "v", ev)
                st.append(p.dma("gpsimd", stash_ap, out_b[:], "dstq"))
            return ev

        for s in range(NCH):
            ts = slice(s * CH3, (s + 1) * CH3)
            cur, nxt = s % 2, 1 - s % 2
            p.wait("sync", "v", vln[0])
            for g in range(G):
                p.dma("sync", Xs[g][:], xin[g][:, ts], "dxl")
            p.wait("vector", "dxl", p.cur("dxl"))
            emit_ln_slab(p, b, 0, out_bf16=Hb, src=Xs, src_is_slab=True, width=CH3)
            vln[0] = p.cur("v")
            if s == 0:
                p.wait("tensor", "v", wdone)
            p.wait("tensor", "v", vln[0])
            p.wait("vector", "dstq", p.cur("dstq"))
            for dg in range(G):
                bi, bank = banks.take("tensor")
                for k in range(G):
                    p.op("tensor", "matmul", bank[:, 0:CH3],
                         WQ[:, (dg * G + k) * 128:(dg * G + k + 1) * 128], Hb[k][:],
                         start=(k == 0), stop=(k == G - 1), sem=("t" if k == G - 1 else None))
                p.wait("vector", "t", p.cur("t"))
                ev = elu_evac(bank, SQ[dg], sq_st[dg][:, ts])
                banks.set_guard(bi, "v", ev)
            for dg in range(G):
                bi, bank = banks.take("tensor")
                for k in range(G):
                    p.op("tensor", "matmul", bank[:, 0:CH3],
                         WK[:, (dg * G + k) * 128:(dg * G + k + 1) * 128], Hb[k][:],
                         start=(k == 0), stop=(k == G - 1), sem=("t" if k == G - 1 else None))
                p.wait("vector", "t", p.cur("t"))
                ev = elu_evac(bank, SK[dg])
                banks.set_guard(bi, "v", ev)
            vk = p.cur("v")
            p.wait("tensor", "v", vk)
            for j in range(NST):
                for half in range(2):
                    bi, bank = banks.take("tensor")
                    for k in range(G):
                        p.op("tensor", "matmul", bank[0:128, :],
                             Hb[k][:, j * 128:(j + 1) * 128],
                             WVT[:, k * C + half * 512:k * C + half * 512 + 512],
                             start=(k == 0), stop=(k == G - 1),
                             sem=("t" if k == G - 1 else None))
                    tmm = p.cur("t")
                    p.wait("vector", "t", tmm)
                    ev = p.op("vector", "tensor_copy",
                              Vt[j][0:128, half * 512:half * 512 + 512], bank[0:128, :])
                    banks.set_guard(bi, "v", ev)
            vv = p.cur("v")
            p.wait("tensor", "v", vv)
            for j in range(NST):
                bi, bank = banks.take("tensor")
                for k in range(G):
                    p.op("tensor", "matmul", bank[:, 0:CH3],
                         SK[k][:, j * 128:(j + 1) * 128], SQ[k][:],
                         start=(k == 0), stop=(k == G - 1), sem=("t" if k == G - 1 else None))
                tmm = p.cur("t")
                p.wait("vector", "t", tmm)
                ev = p.op("vector", "tensor_tensor", Ab[j][:], bank[:, 0:CH3], bcmk[j][:],
                          ALU.mult)
                banks.set_guard(bi, "v", ev)
            for g in range(G):
                p.op("vector", "tensor_copy", Sb[g][:], S[cur][g][:])
            vab = p.cur("v")
            p.wait("tensor", "v", vab)
            for ec in range(G):
                bi, bank = banks.take("tensor")
                for j in range(NST):
                    p.op("tensor", "matmul", bank[:, 0:CH3],
                         Vt[j][:, ec * 128:(ec + 1) * 128], Ab[j][:],
                         start=(j == 0), stop=False)
                for d in range(G):
                    p.op("tensor", "matmul", bank[:, 0:CH3],
                         Sb[d][:, ec * 128:(ec + 1) * 128], SQ[d][:],
                         start=False, stop=(d == G - 1), sem=("t" if d == G - 1 else None))
                tmm = p.cur("t")
                p.wait("vector", "t", tmm)
                orr = orot[ec % 2]
                if odma[ec % 2]:
                    p.wait("vector", f"dsto{ec % 2}", odma[ec % 2])
                ev = p.op("vector", "tensor_copy", orr[:], bank[:, 0:CH3])
                banks.set_guard(bi, "v", ev)
                p.wait("gpsimd", "v", ev)
                odma[ec % 2] = p.dma("gpsimd", o_st[ec][:, ts], orr[:], f"dsto{ec % 2}")
            bi, bank = banks.take("tensor")
            for j in range(NST):
                p.op("tensor", "matmul", bank[0:1, 0:CH3], Vt[j][:, C:E], Ab[j][:],
                     start=(j == 0), stop=False)
            for d in range(G):
                p.op("tensor", "matmul", bank[0:1, 0:CH3], Sb[d][:, C:E], SQ[d][:],
                     start=False, stop=(d == G - 1), sem=("t" if d == G - 1 else None))
            tmm = p.cur("t")
            p.wait("vector", "t", tmm)
            if denw[0]:
                p.wait("vector", "dstd", denw[0])
            evd = p.op("vector", "tensor_copy", denr[:], bank[0:1, 0:CH3])
            banks.set_guard(bi, "v", evd)
            p.wait("gpsimd", "v", evd)
            denw[0] = p.dma("gpsimd", den_st[0:1, ts], denr[:], "dstd")
            for d in range(G):
                p.wait("tensor", "v", p.cur("v"))
                for j in range(NST):
                    p.op("tensor", "matmul", ps_t[:, j * 128:(j + 1) * 128],
                         SK[d][:, j * 128:(j + 1) * 128], ident[:], is_transpose=True,
                         start=True, stop=True, sem=("t" if j == NST - 1 else None))
                tt = p.cur("t")
                p.wait("vector", "t", tt)
                for j in range(NST):
                    p.op("vector", "tensor_copy", SKt[j][:, d * 128:(d + 1) * 128],
                         ps_t[:, j * 128:(j + 1) * 128])
            vskt = p.cur("v")
            p.wait("tensor", "v", vskt)
            for d in range(G):
                for part, (c0, w_) in enumerate(((0, 512), (512, 512), (1024, 1))):
                    bi, bank = banks.take("tensor")
                    for j in range(NST):
                        p.op("tensor", "matmul", bank[:, 0:w_],
                             SKt[j][:, d * 128:(d + 1) * 128], Vt[j][:, c0:c0 + w_],
                             start=(j == 0), stop=(j == NST - 1),
                             sem=("t" if j == NST - 1 else None))
                    tmm = p.cur("t")
                    p.wait("vector", "t", tmm)
                    ev = p.op("vector", "tensor_tensor", S[nxt][d][:, c0:c0 + w_],
                              S[cur][d][:, c0:c0 + w_], bank[:, 0:w_], ALU.add)
                    banks.set_guard(bi, "v", ev)
            vln[0] = p.cur("v")
        fin = NCH % 2
        vf = p.cur("v")
        p.wait("gpsimd", "v", vf)
        for g in range(G):
            p.dma("gpsimd", s_out[g], S[fin][g][:], "dsts")
        p.wait("gpsimd", "dsts", p.cur("dsts"))
        p.wait("gpsimd", "dstq", p.cur("dstq"))
        p.wait("gpsimd", "dstd", p.cur("dstd"))
        p.wait_all("gpsimd", "dsto", 2)
        p.build(es)
    return nc



def build_l3a2():
    nc = bass.Bass("TRN2", target_bir_lowering=False)
    E = C + 1
    xin = nc.dram_tensor("xin", [G, 128, T], F32, kind="ExternalInput").ap()
    s_prev = nc.dram_tensor("s_prev", [G, 128, E], F32, kind="ExternalInput").ap()
    sq_st = nc.dram_tensor("sq_st", [G, 128, T], BF16, kind="ExternalInput").ap()
    o_st = nc.dram_tensor("o_st", [G, 128, T], F32, kind="ExternalInput").ap()
    den_st = nc.dram_tensor("den_st", [1, T], F32, kind="ExternalInput").ap()
    w_m = nc.dram_tensor("w_m", [128, G, G, 128], F32, kind="ExternalInput").ap()
    mn_b = nc.dram_tensor("mn_b", [128, G, 128], F32, kind="ExternalInput").ap()
    mn_v = nc.dram_tensor("mn_v", [128, G], F32, kind="ExternalInput").ap()
    w_o = nc.dram_tensor("w_o", [128, G, G, 128], F32, kind="ExternalInput").ap()
    gate_b = nc.dram_tensor("gate_b", [128, 1], F32, kind="ExternalInput").ap()
    xout = nc.dram_tensor("xout", [G, 128, T], F32, kind="ExternalOutput").ap()

    es = ExitStack()
    with es:
        p = Prog(nc)
        b = alloc_common(nc, es, want_X=False, n_hs=2)
        WM = es.enter_context(nc.sbuf_tensor("WM", [128, G * G * 128], BF16))
        WO = es.enter_context(nc.sbuf_tensor("WO", [128, G * G * 128], BF16))
        MNB = es.enter_context(nc.sbuf_tensor("MNB", [128, G * 128], BF16))
        mnv = es.enter_context(nc.sbuf_tensor("mnv", [128, G], F32))
        mnb16 = es.enter_context(nc.sbuf_tensor("mnb16", [128, G], BF16))
        Sb = [es.enter_context(nc.sbuf_tensor(f"Sb{g}", [128, E], BF16)) for g in range(G)]
        srot = es.enter_context(nc.sbuf_tensor("srot", [128, E], F32))
        SQ = [es.enter_context(nc.sbuf_tensor(f"SQ{g}", [128, SLAB], BF16)) for g in range(G)]
        Xs = [es.enter_context(nc.sbuf_tensor(f"Xs{g}", [128, SLAB], F32)) for g in range(G)]
        ob = [es.enter_context(nc.sbuf_tensor(f"obA{g}", [128, SLAB], BF16)) for g in range(G)]
        orot = [es.enter_context(nc.sbuf_tensor(f"orot{i}", [128, SLAB], F32)) for i in range(2)]
        gcol = es.enter_context(nc.sbuf_tensor("gcol", [128, 1], F32))
        acl = es.enter_context(nc.sbuf_tensor("acl", [128, 1], F32))
        denf = es.enter_context(nc.sbuf_tensor("denf", [1, SLAB], F32))
        dent = es.enter_context(nc.sbuf_tensor("dent", [1, SLAB], F32))
        onesr = es.enter_context(nc.sbuf_tensor("onesr", [1, 128], F32))
        rdenb = es.enter_context(nc.sbuf_tensor("rdenb", [128, SLAB], F32))
        recip = es.enter_context(nc.sbuf_tensor("recip", [128, SLAB], F32))
        sc1 = es.enter_context(nc.sbuf_tensor("scr1", [128, SLAB], F32))
        sc2 = es.enter_context(nc.sbuf_tensor("scr2", [128, SLAB], F32))
        Xn = [es.enter_context(nc.sbuf_tensor(f"Xn{i}", [128, SLAB], F32)) for i in range(2)]

        p.op("vector", "memset", b["onesb"][:], 1.0)
        p.op("vector", "memset", onesr[:], 1.0)
        p.dma("sync", gcol[:], gate_b[:, :], "dsm")
        p.dma("sync", mnv[:], mn_v[:, :], "dsm")
        p.wait("scalar", "dsm", p.cur("dsm"))
        sg = p.op("scalar", "activation", b["stat"][0][:, 0:1], gcol[:], AF.Sigmoid)
        p.wait("vector", "s", sg)
        p.op("vector", "tensor_copy", gcol[:], b["stat"][0][:, 0:1])
        p.op("vector", "tensor_copy", mnb16[:], mnv[:])
        vmn = p.cur("v")
        p.wait("tensor", "v", vmn)
        p.op("tensor", "matmul", b["psum"][0][:, 0:G], b["onesb"][:], mnb16[:],
             start=True, stop=True, sem="t")
        p.wait("vector", "t", p.cur("t"))
        p.op("vector", "reduce_sum", b["stat"][0][:, 0:1], b["psum"][0][:, 0:G], axis=AXX)
        vacl = p.op("vector", "tensor_scalar", acl[:], b["stat"][0][:, 0:1], 1e-6, None,
                    ALU.is_ge)
        for g in range(G):
            p.dma("sync", srot[:], s_prev[g], "dsp")
            p.wait("vector", "dsp", p.cur("dsp"))
            vcp = p.op("vector", "tensor_copy", Sb[g][:], srot[:])
            p.wait("sync", "v", vcp)
        wl = WeightLoader(nc, es, p, "w")
        wdone = wl.load(w_m.rearrange("p a b c -> p (a b c)"), WM, 0, G * G * 128)
        wl.load(mn_b.rearrange("p a c -> p (a c)"), MNB, 0, G * 128)
        wdone = wl.load(w_o.rearrange("p a b c -> p (a b c)"), WO, 0, G * G * 128)

        banks = Banks(p, b["psum"], range(0, 8))
        banks.set_guard(0, "v", vacl)
        xo = []
        vsl = [0]
        ovr = [0, 0]
        for s in range(NSLAB):
            ts = slice(s * SLAB, (s + 1) * SLAB)
            p.wait("sync", "v", vsl[0])
            for g in range(G):
                p.dma("sync", SQ[g][:], sq_st[g][:, ts], "dsq")
                p.dma("sync", Xs[g][:], xin[g][:, ts], "dsq")
            p.dma("sync", dent[:], den_st[0:1, ts], "dsq")
            dsq = p.cur("dsq")
            p.wait("tensor", "dsq", dsq)
            if s == 0:
                p.wait("tensor", "v", wdone)
            bi_d, bank_d = banks.take("tensor")
            for d in range(G):
                p.op("tensor", "matmul", bank_d[0:1, :], Sb[d][:, C:E], SQ[d][:],
                     start=(d == 0), stop=(d == G - 1), sem=("t" if d == G - 1 else None))
            tdn = p.cur("t")
            p.wait("vector", "t", tdn)
            p.wait("vector", "dsq", dsq)
            p.op("vector", "tensor_tensor", denf[:], dent[:], bank_d[0:1, :], ALU.add)
            vdf = p.op("vector", "tensor_scalar_max", dent[:], denf[:], 1e-6)
            banks.set_guard(bi_d, "v", vdf)
            p.wait("scalar", "v", vdf)
            p.op("scalar", "activation", denf[:], dent[:], AF.Ln)
            sdr = p.op("scalar", "activation", dent[:], denf[:], AF.Exp, scale=-1.0)
            bi_r, bank_r = banks.take("tensor")
            p.wait("tensor", "s", sdr)
            p.op("tensor", "matmul", bank_r[:], onesr[:], dent[:], start=True, stop=True,
                 sem="t")
            trb = p.cur("t")
            p.wait("vector", "t", trb)
            vrb = p.op("vector", "tensor_copy", rdenb[:], bank_r[:])
            banks.set_guard(bi_r, "v", vrb)
            bi_n, bank_n = banks.take("tensor")
            for k in range(G):
                p.op("tensor", "matmul", bank_n[:], MNB[:, k * 128:(k + 1) * 128],
                     SQ[k][:], start=(k == 0), stop=(k == G - 1),
                     sem=("t" if k == G - 1 else None))
            tno = p.cur("t")
            p.wait("vector", "t", tno)
            vnc = p.op("vector", "tensor_scalar_max", sc1[:], bank_n[:], 1e-6)
            banks.set_guard(bi_n, "v", vnc)
            p.wait("scalar", "v", vnc)
            p.op("scalar", "activation", sc2[:], sc1[:], AF.Ln)
            snr = p.op("scalar", "activation", recip[:], sc2[:], AF.Exp, scale=-1.0)
            p.wait("vector", "s", snr)
            for ec in range(G):
                par = ec % 2
                if ovr[par]:
                    p.wait("scalar", "v", ovr[par])
                p.dma("scalar", orot[par][:], o_st[ec][:, ts], "dol")
                bi_o, bank_o = banks.take("tensor")
                for d in range(G):
                    p.op("tensor", "matmul", bank_o[:], Sb[d][:, ec * 128:(ec + 1) * 128],
                         SQ[d][:], start=(d == 0), stop=(d == G - 1),
                         sem=("t" if d == G - 1 else None))
                to = p.cur("t")
                bi_m, bank_m = banks.take("tensor")
                for k in range(G):
                    p.op("tensor", "matmul", bank_m[:],
                         WM[:, (ec * G + k) * 128:(ec * G + k + 1) * 128], SQ[k][:],
                         start=(k == 0), stop=(k == G - 1), sem=("t" if k == G - 1 else None))
                tm = p.cur("t")
                p.wait("vector", "t", tm)
                p.wait("vector", "dol", p.cur("dol"))
                ovr[par] = p.op("vector", "tensor_tensor", sc1[:], bank_o[:], orot[par][:],
                                ALU.add)
                vloc = p.op("vector", "tensor_tensor", sc2[:], sc1[:], rdenb[:], ALU.mult)
                banks.set_guard(bi_o, "v", vloc)
                p.op("vector", "scalar_tensor_tensor", sc1[:], bank_m[:], acl[:, 0:1],
                     recip[:], ALU.mult, ALU.mult)
                banks.set_guard(bi_m, "v", p.cur("v"))
                p.op("vector", "tensor_tensor", b["Hs"][0][:], sc1[:], sc2[:], ALU.subtract)
                vo = p.op("vector", "scalar_tensor_tensor", ob[ec][:], b["Hs"][0][:],
                          gcol[:, 0:1], sc2[:], ALU.mult, ALU.add)
            vsl[0] = p.cur("v")
            p.wait("tensor", "v", vsl[0])
            for dg in range(G):
                bi, bank = banks.take("tensor")
                for k in range(G):
                    p.op("tensor", "matmul", bank[:],
                         WO[:, (dg * G + k) * 128:(dg * G + k + 1) * 128], ob[k][:],
                         start=(k == 0), stop=(k == G - 1), sem=("t" if k == G - 1 else None))
                tmm = p.cur("t")
                p.wait("vector", "t", tmm)
                xnp = len(xo) % 2
                if len(xo) >= 2:
                    p.wait("vector", "dso", xo[-2])
                ev = p.op("vector", "tensor_tensor", Xn[xnp][:], bank[:], Xs[dg][:], ALU.add)
                banks.set_guard(bi, "v", ev)
                p.wait("gpsimd", "v", ev)
                xo.append(p.dma("gpsimd", xout[dg][:, ts], Xn[xnp][:], "dso"))
            vsl[0] = p.cur("v")
        p.wait("gpsimd", "dso", p.cur("dso"))
        p.build(es)
    return nc



def build_l2a1():
    nc = bass.Bass("TRN2", target_bir_lowering=False)
    NH, DK, DV = 4, 128, 256
    CK = 32
    TK = 128
    xin = nc.dram_tensor("xin", [G, 128, T], F32, kind="ExternalInput").ap()
    xtail = nc.dram_tensor("xtail", [128, G], F32, kind="ExternalInput").ap()
    has_tail = nc.dram_tensor("has_tail", [128, 1], F32, kind="ExternalInput").ap()
    w_x0 = nc.dram_tensor("w_x0", [128, 2, G, 128], F32, kind="ExternalInput").ap()
    w_x2 = nc.dram_tensor("w_x2", [32, 5, G, 128], F32, kind="ExternalInput").ap()
    mu_0 = nc.dram_tensor("mu_0", [128, G], F32, kind="ExternalInput").ap()
    x_b = nc.dram_tensor("x_b", [128, 5 * G], F32, kind="ExternalInput").ap()
    w_r = nc.dram_tensor("w_r", [128, 4, G, 128], F32, kind="ExternalInput").ap()
    w_w1 = nc.dram_tensor("w_w1", [128, 1, G, 128], F32, kind="ExternalInput").ap()
    w_w2 = nc.dram_tensor("w_w2", [64, 4, 128], F32, kind="ExternalInput").ap()
    b_w2 = nc.dram_tensor("b_w2", [128, 4], F32, kind="ExternalInput").ap()
    w_k = nc.dram_tensor("w_k", [128, 4, G, 128], F32, kind="ExternalInput").ap()
    w_vt = nc.dram_tensor("w_vt", [128, G, C], F32, kind="ExternalInput").ap()
    w_g = nc.dram_tensor("w_g", [128, G, G, 128], F32, kind="ExternalInput").ap()
    u_h = nc.dram_tensor("u_h", [128, 4], F32, kind="ExternalInput").ap()
    idm = nc.dram_tensor("idm", [128, 128], F32, kind="ExternalInput").ap()
    msk1 = nc.dram_tensor("msk1", [4, 128, 32], F32, kind="ExternalInput").ap()
    msk2 = nc.dram_tensor("msk2", [4, 128, 32], F32, kind="ExternalInput").ap()
    o_st = nc.dram_tensor("o_st", [G, 128, T], BF16, kind="ExternalOutput").ap()
    qg_st = nc.dram_tensor("qg_st", [4, 128, T], BF16, kind="ExternalOutput").ap()
    sg_st = nc.dram_tensor("sg_st", [G, 128, T], BF16, kind="ExternalOutput").ap()
    s_half = nc.dram_tensor("s_half", [4, 128, DV], F32, kind="ExternalOutput").ap()

    es = ExitStack()
    with es:
        p = Prog(nc)
        b = alloc_common(nc, es, want_X=False, n_hs=3, n_psum=7)
        ps_t = es.enter_context(nc.psum_tensor("ps_t", [128, 1024], BF16))
        WX0 = es.enter_context(nc.sbuf_tensor("WX0", [128, 2 * G * 128], BF16))
        WX2 = es.enter_context(nc.sbuf_tensor("WX2", [32, 5 * G * 128], BF16))
        WR = es.enter_context(nc.sbuf_tensor("WRl2", [128, 4 * G * 128], BF16))
        WW1 = es.enter_context(nc.sbuf_tensor("WW1", [128, G * 128], BF16))
        WW2 = es.enter_context(nc.sbuf_tensor("WW2", [64, 4 * 128], BF16))
        WKl = es.enter_context(nc.sbuf_tensor("WKl2", [128, 4 * G * 128], BF16))
        WVT = es.enter_context(nc.sbuf_tensor("WVTl2", [128, G * C], BF16))
        WG = es.enter_context(nc.sbuf_tensor("WGl2", [128, G * G * 128], BF16))
        mu0 = es.enter_context(nc.sbuf_tensor("mu0", [128, G], F32))
        xbias = es.enter_context(nc.sbuf_tensor("xbias", [128, 5 * G], F32))
        bw2 = es.enter_context(nc.sbuf_tensor("bw2", [128, 4], F32))
        ucol = es.enter_context(nc.sbuf_tensor("ucol", [128, 4], F32))
        hasT = es.enter_context(nc.sbuf_tensor("hasT", [128, 1], F32))
        tailf = es.enter_context(nc.sbuf_tensor("tailf", [128, G], F32))
        Hcol = es.enter_context(nc.sbuf_tensor("HcolA", [128, G], F32))
        Hcol2 = es.enter_context(nc.sbuf_tensor("HcolA2", [128, G], F32))
        ident = es.enter_context(nc.sbuf_tensor("identA", [128, 128], BF16))
        m1k = [es.enter_context(nc.sbuf_tensor(f"m1k{c}", [128, 32], U8)) for c in range(4)]
        m2k = [es.enter_context(nc.sbuf_tensor(f"m2k{c}", [128, 32], U8)) for c in range(4)]
        zb32 = es.enter_context(nc.sbuf_tensor("zb32", [128, 32], BF16))
        Xs = [es.enter_context(nc.sbuf_tensor(f"Xs{g}", [128, SLAB], F32)) for g in range(G)]
        HL = [es.enter_context(nc.sbuf_tensor(f"HLa{g}", [128, SLAB], F32)) for g in range(G)]
        dsh = [es.enter_context(nc.sbuf_tensor(f"dsh{g}", [128, SLAB], BF16)) for g in range(G)]
        mxb = [es.enter_context(nc.sbuf_tensor(f"mxb{g}", [128, SLAB], BF16)) for g in range(G)]
        musf = es.enter_context(nc.sbuf_tensor("musf", [128, SLAB], F32))
        xpb = [es.enter_context(nc.sbuf_tensor(f"xpb{i}", [128, SLAB], BF16)) for i in range(2)]
        xpn = es.enter_context(nc.sbuf_tensor("xpn", [32, 5 * SLAB], BF16))
        rb = es.enter_context(nc.sbuf_tensor("rbl2", [128, 4 * SLAB], BF16))
        kb = es.enter_context(nc.sbuf_tensor("kbl2", [128, 4 * SLAB], BF16))
        rub = es.enter_context(nc.sbuf_tensor("rub", [128, 4 * SLAB], BF16))
        wnf = es.enter_context(nc.sbuf_tensor("wnf", [128, 4 * SLAB], F32))
        cumP = es.enter_context(nc.sbuf_tensor("cumP", [128, 4 * (SLAB + 1)], F32))
        onesf = es.enter_context(nc.sbuf_tensor("onesfA", [128, SLAB], F32))
        vt = [es.enter_context(nc.sbuf_tensor(f"vtA{i}", [128, C], BF16)) for i in range(4)]
        ktl = es.enter_context(nc.sbuf_tensor("ktl", [128, 128], BF16))
        qt32 = es.enter_context(nc.sbuf_tensor("qt32", [128, 32], BF16))
        qtt = es.enter_context(nc.sbuf_tensor("qtt", [128, 32], BF16))
        kot = es.enter_context(nc.sbuf_tensor("kot", [128, 512], BF16))
        kof = es.enter_context(nc.sbuf_tensor("kof", [128, 128], BF16))
        S = [[es.enter_context(nc.sbuf_tensor(f"SA{i}{h}", [128, DV], F32)) for h in range(NH)]
             for i in range(2)]
        Sb = [es.enter_context(nc.sbuf_tensor(f"SbA{h}", [128, DV], BF16)) for h in range(NH)]
        sc32 = es.enter_context(nc.sbuf_tensor("sc32", [128, 32], BF16))
        sf1 = es.enter_context(nc.sbuf_tensor("sf1", [128, 32], BF16))
        sf2 = es.enter_context(nc.sbuf_tensor("sf2", [128, 32], BF16))
        sfb = es.enter_context(nc.sbuf_tensor("sfb", [128, 32], BF16))
        ol = [es.enter_context(nc.sbuf_tensor(f"olA{g}", [128, SLAB], BF16)) for g in range(G)]
        e1 = b["Hs"][1]
        e2 = b["Hs"][2]
        qgr = es.enter_context(nc.sbuf_tensor("qgr", [128, SLAB], BF16))
        sgr = [es.enter_context(nc.sbuf_tensor("sgr0", [128, SLAB], BF16))] * 2
        carry = es.enter_context(nc.sbuf_tensor("carryA", [128, 8], F32))

        p.op("vector", "memset", b["onesb"][:], 1.0)
        p.op("vector", "memset", b["epsc"][:], LN_EPS)
        p.op("vector", "memset", onesf[:], 1.0)
        p.op("vector", "memset", zb32[:], 0.0)
        p.op("vector", "memset", carry[:], 0.0)
        for h in range(NH):
            p.op("vector", "memset", S[0][h][:], 0.0)
        p.dma("sync", mu0[:], mu_0[:, :], "dsm")
        p.dma("sync", xbias[:], x_b[:, :], "dsm")
        p.dma("sync", bw2[:], b_w2[:, :], "dsm")
        p.dma("sync", ucol[:], u_h[:, :], "dsm")
        p.dma("sync", hasT[:], has_tail[:, :], "dsm")
        p.dma("sync", tailf[:], xtail[:, :], "dsm")
        sm = p.cur("dsm")
        p.wait("vector", "dsm", sm)
        p.dma("sync", b["stat"][0][:, 0:128], idm[:, :], "dsm2")
        p.wait("vector", "dsm2", p.cur("dsm2"))
        vid = p.op("vector", "tensor_copy", ident[:], b["stat"][0][:, 0:128])
        p.wait("sync", "v", vid)
        for c in range(4):
            p.dma("sync", b["stat"][0][:, 0:32], msk1[c], "dsm3")
            p.dma("sync", b["stat"][0][:, 32:64], msk2[c], "dsm3")
            p.wait("vector", "dsm3", p.cur("dsm3"))
            p.op("vector", "tensor_copy", m1k[c][:], b["stat"][0][:, 0:32])
            vmk = p.op("vector", "tensor_copy", m2k[c][:], b["stat"][0][:, 32:64])
            p.wait("sync", "v", vmk)
        wl = WeightLoader(nc, es, p, "w")
        wl.load(w_x0.rearrange("p a b c -> p (a b c)"), WX0, 0, 2 * G * 128)
        wx2f = w_x2.rearrange("p a b c -> p (a b c)")
        p.wait("sync", "v", p.cur("v"))
        wx2_cast = []
        for i in range(10):
            par = (wl.seq + i) % 2
            if i >= 2:
                p.wait("sync", "v", wx2_cast[i - 2])
            lt = p.dma("sync", wl.stg[par][0:32, 0:512], wx2f[:, i * 512:(i + 1) * 512],
                       f"wlx{par}")
            p.wait("vector", f"wlx{par}", lt)
            wx2_cast.append(p.op("vector", "tensor_copy",
                                 WX2[0:32, i * 512:(i + 1) * 512], wl.stg[par][0:32, 0:512]))
        wl.cast_tgts.extend(wx2_cast)
        wl.seq += 10
        wl.load(w_r.rearrange("p a b c -> p (a b c)"), WR, 0, 4 * G * 128)
        wl.load(w_w1.rearrange("p a b c -> p (a b c)"), WW1, 0, G * 128)
        ww2f = w_w2.rearrange("p a c -> p (a c)")
        par2 = wl.seq % 2
        p.wait("sync", "v", wl.cast_tgts[wl.seq - 2])
        lt = p.dma("sync", wl.stg[par2][0:64, 0:512], ww2f[:, 0:512], f"wlx{par2}")
        p.wait("vector", f"wlx{par2}", p.cur(f"wlx{par2}"))
        vww2 = p.op("vector", "tensor_copy", WW2[0:64, 0:512], wl.stg[par2][0:64, 0:512])
        wl.cast_tgts.append(vww2)
        wl.seq += 1
        wl.load(w_k.rearrange("p a b c -> p (a b c)"), WKl, 0, 4 * G * 128)
        wl.load(w_vt.rearrange("p a c -> p (a c)"), WVT, 0, G * C)
        wdone = wl.load(w_g.rearrange("p a b c -> p (a b c)"), WG, 0, G * G * 128)

        p.wait("vector", "dsm", sm)
        emit_ln_cols(p, b, tailf, Hcol, G, mask_col=hasT[:, 0:1])

        banks = Banks(p, b["psum"], range(2, 7))
        vln = [0]
        sgw = [0, 0]
        ow = [0, 0]
        qgw = [0]

        def mixed_phase(n, dst_tiles):
            p.wait("tensor", "v", p.cur("v"))
            for dg in range(G):
                bi, bank = banks.take("tensor")
                p.op("tensor", "matmul", bank[:],
                     WX2[0:32, (n * G + dg) * 128:(n * G + dg + 1) * 128],
                     xpn[0:32, n * SLAB:(n + 1) * SLAB], start=True, stop=True, sem="t")
                tmm = p.cur("t")
                p.wait("vector", "t", tmm)
                p.op("vector", "tensor_scalar", musf[:], bank[:],
                     xbias[:, n * G + dg:n * G + dg + 1], None, ALU.add)
                p.op("vector", "tensor_tensor", b["Hs"][0][:], musf[:], dsh[dg][:], ALU.mult)
                ev = p.op("vector", "tensor_tensor", dst_tiles[dg][:], b["Hs"][0][:],
                          HL[dg][:], ALU.add)
                banks.set_guard(bi, "v", ev)

        pcur = [0]
        for s in range(NSLAB):
            ts = slice(s * SLAB, (s + 1) * SLAB)
            Hc = Hcol if s % 2 == 0 else Hcol2
            Hc_next = Hcol2 if s % 2 == 0 else Hcol
            p.wait("sync", "v", vln[0])
            for g in range(G):
                p.dma("sync", Xs[g][:], xin[g][:, ts], "dxl")
            p.wait("vector", "dxl", p.cur("dxl"))
            emit_ln_slab(p, b, 0, out_fp32=HL, src=Xs, src_is_slab=True)
            for g in range(G):
                p.op("vector", "tensor_tensor", dsh[g][:, 1:SLAB], HL[g][:, 0:SLAB - 1],
                     HL[g][:, 1:SLAB], ALU.subtract)
                p.op("vector", "tensor_tensor", dsh[g][:, 0:1], Hc[:, g:g + 1],
                     HL[g][:, 0:1], ALU.subtract)
            for g in range(G):
                p.op("vector", "tensor_copy", Hc_next[:, g:g + 1], HL[g][:, SLAB - 1:SLAB])
            for g in range(G):
                p.op("vector", "scalar_tensor_tensor", mxb[g][:], dsh[g][:],
                     mu0[:, g:g + 1], HL[g][:], ALU.mult, ALU.add)
            vmx = p.cur("v")
            if s == 0:
                p.wait("tensor", "v", wdone)
            p.wait("tensor", "v", vmx)
            for dg in range(2):
                bi, bank = banks.take("tensor")
                for k in range(G):
                    p.op("tensor", "matmul", bank[:],
                         WX0[:, (dg * G + k) * 128:(dg * G + k + 1) * 128], mxb[k][:],
                         start=(k == 0), stop=(k == G - 1), sem=("t" if k == G - 1 else None))
                tmm = p.cur("t")
                p.wait("scalar", "t", tmm)
                sev = p.op("scalar", "activation", xpb[dg][:], bank[:], AF.Tanh)
                banks.set_guard(bi, "s", sev)
            p.wait("vector", "s", p.cur("s"))
            for n in range(5):
                xt = 0 if n < 3 else 1
                off = (n % 3) * 32 if n < 3 else (n - 3) * 32
                p.op("vector", "tensor_copy", xpn[0:32, n * SLAB:(n + 1) * SLAB],
                     xpb[xt][off:off + 32, :])
            mixed_phase(0, mxb)
            vfin = p.cur("v")
            p.wait("tensor", "v", vfin)
            for hd in range(4):
                bi, bank = banks.take("tensor")
                for k in range(G):
                    p.op("tensor", "matmul", bank[:],
                         WR[:, (hd * G + k) * 128:(hd * G + k + 1) * 128], mxb[k][:],
                         start=(k == 0), stop=(k == G - 1), sem=("t" if k == G - 1 else None))
                tmm = p.cur("t")
                p.wait("vector", "t", tmm)
                p.op("vector", "tensor_copy", rb[:, hd * SLAB:(hd + 1) * SLAB], bank[:])
                ev = p.op("vector", "tensor_scalar", rub[:, hd * SLAB:(hd + 1) * SLAB],
                          bank[:], ucol[:, hd:hd + 1], None, ALU.mult)
                banks.set_guard(bi, "v", ev)
            mixed_phase(1, mxb)
            vfin = p.cur("v")
            p.wait("tensor", "v", vfin)
            bi1, bank1 = banks.take("tensor")
            for k in range(G):
                p.op("tensor", "matmul", bank1[:],
                     WW1[:, k * 128:(k + 1) * 128], mxb[k][:],
                     start=(k == 0), stop=(k == G - 1), sem=("t" if k == G - 1 else None))
            tmm = p.cur("t")
            p.wait("scalar", "t", tmm)
            sev = p.op("scalar", "activation", xpb[0][:], bank1[:], AF.Tanh)
            banks.set_guard(bi1, "s", sev)
            p.wait("tensor", "s", sev)
            wnf_v = 0
            for hd in range(4):
                bi, bank = banks.take("tensor")
                p.op("tensor", "matmul", bank[:], WW2[0:64, hd * 128:(hd + 1) * 128],
                     xpb[0][0:64, :], start=True, stop=True, sem="t")
                tmm = p.cur("t")
                p.wait("scalar", "t", tmm)
                p.wait("scalar", "v", wnf_v)
                p.op("scalar", "activation", musf[:], bank[:], AF.Exp,
                     bias=bw2[:, hd:hd + 1])
                sv2 = p.cur("s")
                p.wait("vector", "s", sv2)
                ev = p.op("vector", "tensor_scalar", wnf[:, hd * SLAB:(hd + 1) * SLAB],
                          musf[:], -1.0, None, ALU.mult)
                wnf_v = ev
                banks.set_guard(bi, "s", sv2)
            for hd in range(4):
                c0 = hd * (SLAB + 1)
                p.op("vector", "memset", cumP[:, c0:c0 + 1], 0.0)
                p.op("vector", "tensor_tensor_scan", cumP[:, c0 + 1:c0 + 1 + SLAB],
                     onesf[:], wnf[:, hd * SLAB:(hd + 1) * SLAB], 0.0, ALU.mult, ALU.add)
            vcm = p.cur("v")
            for hd in range(4):
                c0 = hd * (SLAB + 1)
                cr, crn = (hd, 4 + hd) if s % 2 == 0 else (4 + hd, hd)
                va = p.op("vector", "tensor_scalar", e1[:], cumP[:, c0:c0 + SLAB],
                          carry[:, cr:cr + 1], None, ALU.add)
                p.wait("scalar", "v", va)
                se = p.op("scalar", "activation", e2[:], e1[:], AF.Exp)
                p.wait("vector", "s", se)
                if qgw[0]:
                    p.wait("vector", "dqg", qgw[0])
                vq = p.op("vector", "tensor_tensor", qgr[:],
                          rb[:, hd * SLAB:(hd + 1) * SLAB], e2[:], ALU.mult)
                p.wait("gpsimd", "v", vq)
                qgw[0] = p.dma("gpsimd", qg_st[hd][:, ts], qgr[:], "dqg")
                p.op("vector", "tensor_scalar", carry[:, crn:crn + 1],
                     cumP[:, c0 + SLAB:c0 + SLAB + 1], carry[:, cr:cr + 1], None, ALU.add)
            mixed_phase(2, mxb)
            vfin = p.cur("v")
            p.wait("tensor", "v", vfin)
            for hd in range(4):
                bi, bank = banks.take("tensor")
                for k in range(G):
                    p.op("tensor", "matmul", bank[:],
                         WKl[:, (hd * G + k) * 128:(hd * G + k + 1) * 128], mxb[k][:],
                         start=(k == 0), stop=(k == G - 1), sem=("t" if k == G - 1 else None))
                tmm = p.cur("t")
                p.wait("vector", "t", tmm)
                ev = p.op("vector", "tensor_copy", kb[:, hd * SLAB:(hd + 1) * SLAB], bank[:])
                banks.set_guard(bi, "v", ev)
            mixed_phase(3, mxb)
            vfin = p.cur("v")
            p.wait("tensor", "v", vfin)
            for tt in range(4):
                for half in range(2):
                    bi, bank = banks.take("tensor")
                    for k in range(G):
                        p.op("tensor", "matmul", bank[0:128, :],
                             mxb[k][:, tt * 128:(tt + 1) * 128],
                             WVT[:, k * C + half * 512:k * C + half * 512 + 512],
                             start=(k == 0), stop=(k == G - 1),
                             sem=("t" if k == G - 1 else None))
                    tmm = p.cur("t")
                    p.wait("vector", "t", tmm)
                    ev = p.op("vector", "tensor_copy",
                              vt[tt][:, half * 512:half * 512 + 512], bank[0:128, :])
                    banks.set_guard(bi, "v", ev)
            mixed_phase(4, mxb)
            vfin = p.cur("v")
            p.wait("tensor", "v", vfin)
            for dg in range(G):
                bi, bank = banks.take("tensor")
                for k in range(G):
                    p.op("tensor", "matmul", bank[:],
                         WG[:, (dg * G + k) * 128:(dg * G + k + 1) * 128], mxb[k][:],
                         start=(k == 0), stop=(k == G - 1), sem=("t" if k == G - 1 else None))
                tmm = p.cur("t")
                p.wait("scalar", "t", tmm)
                if sgw[0]:
                    p.wait("scalar", "dsg", sgw[0])
                sev = p.op("scalar", "activation", sgr[0][:], bank[:], AF.Silu)
                banks.set_guard(bi, "s", sev)
                p.wait("gpsimd", "s", sev)
                sgw[0] = p.dma("gpsimd", sg_st[dg][:, ts], sgr[0][:], "dsg")

            p.wait("vector", "dol", p.cur("dol"))
            for hd in range(NH):
                cur, nxt = pcur[0], 1 - pcur[0]
                c0 = hd * (SLAB + 1)
                hs, he = hd * SLAB, (hd + 1) * SLAB
                for tt in range(4):
                    tb0 = tt * 128
                    va = p.op("vector", "tensor_scalar", e1[:, 0:128],
                              cumP[:, c0 + 1 + tb0:c0 + 1 + tb0 + 128],
                              cumP[:, c0 + tb0 + 128:c0 + tb0 + 129], None, ALU.subtract)
                    p.wait("scalar", "v", va)
                    se = p.op("scalar", "activation", e2[:, 0:128], e1[:, 0:128], AF.Exp,
                              scale=-1.0)
                    p.wait("vector", "s", se)
                    vko = p.op("vector", "tensor_tensor", kof[:],
                               kb[:, hs + tb0:hs + tb0 + 128], e2[:, 0:128], ALU.mult)
                    p.wait("tensor", "v", vko)
                    p.op("tensor", "matmul", ps_t[:, hd * 128:(hd + 1) * 128], kof[:],
                         ident[:], is_transpose=True, start=True, stop=True, sem="t")
                    ttr = p.cur("t")
                    p.wait("vector", "t", ttr)
                    vkt = p.op("vector", "tensor_copy", kot[:, hd * 128:(hd + 1) * 128],
                               ps_t[:, hd * 128:(hd + 1) * 128])
                    vsb = p.op("vector", "tensor_copy", Sb[hd][:], S[cur][hd][:])
                    for ck in range(4):
                        cb = tb0 + ck * 32
                        va = p.op("vector", "tensor_scalar", e1[:, 0:32],
                                  cumP[:, c0 + cb:c0 + cb + 32],
                                  cumP[:, c0 + cb:c0 + cb + 1], None, ALU.subtract)
                        p.wait("scalar", "v", va)
                        se = p.op("scalar", "activation", e2[:, 0:32], e1[:, 0:32], AF.Exp)
                        p.wait("vector", "s", se)
                        p.op("vector", "tensor_tensor", qt32[:],
                             rb[:, hs + cb:hs + cb + 32], e2[:, 0:32], ALU.mult)
                        va = p.op("vector", "tensor_scalar", e1[:, 32:33],
                                  cumP[:, c0 + cb:c0 + cb + 1],
                                  cumP[:, c0 + tb0:c0 + tb0 + 1], None, ALU.subtract)
                        p.wait("scalar", "v", va)
                        se = p.op("scalar", "activation", e2[:, 32:33], e1[:, 32:33], AF.Exp)
                        p.wait("vector", "s", se)
                        vq = p.op("vector", "tensor_scalar", qtt[:], qt32[:],
                                  e2[:, 32:33], None, ALU.mult)
                        va = p.op("vector", "tensor_scalar", e1[:, 0:128],
                                  cumP[:, c0 + 1 + tb0:c0 + 1 + tb0 + 128],
                                  cumP[:, c0 + cb + 32:c0 + cb + 33],
                                  cumP[:, c0 + cb:c0 + cb + 1], ALU.max, ALU.subtract)
                        p.wait("scalar", "v", va)
                        se = p.op("scalar", "activation", e2[:, 0:128], e1[:, 0:128],
                                  AF.Exp, scale=-1.0)
                        p.wait("vector", "s", se)
                        vk = p.op("vector", "tensor_tensor", ktl[:],
                                  kb[:, hs + tb0:hs + tb0 + 128], e2[:, 0:128], ALU.mult)
                        bi, bank = banks.take("tensor")
                        p.wait("tensor", "v", vk)
                        p.op("tensor", "matmul", bank[:, 0:32], ktl[:], qt32[:],
                             start=True, stop=True)
                        p.op("tensor", "matmul", bank[:, 32:64],
                             kb[:, hs + tb0:hs + tb0 + 128],
                             rub[:, hs + cb:hs + cb + 32], start=True, stop=True, sem="t")
                        tms = p.cur("t")
                        p.wait("vector", "t", tms)
                        p.op("vector", "tensor_copy", sf1[:], bank[:, 0:32])
                        vv2 = p.op("vector", "tensor_copy", sf2[:], bank[:, 32:64])
                        banks.set_guard(bi, "v", vv2)
                        p.op("vector", "select", sfb[:], m2k[ck][:], sf2[:], zb32[:],
                             add_drain=True)
                        p.op("vector", "tensor_copy", sc32[:], sfb[:])
                        vsel = p.op("vector", "select", sfb[:], m1k[ck][:], sf1[:],
                                    sc32[:], add_drain=True)
                        bio, banko = banks.take("tensor")
                        p.wait("tensor", "v", vsel)
                        for half in range(2):
                            p.op("tensor", "matmul", banko[:, half * 32:half * 32 + 32],
                                 vt[tt][0:cb - tb0 + 32, hd * 256 + half * 128:
                                        hd * 256 + half * 128 + 128],
                                 sfb[0:cb - tb0 + 32, :], start=True, stop=False)
                            p.op("tensor", "matmul", banko[:, half * 32:half * 32 + 32],
                                 Sb[hd][:, half * 128:half * 128 + 128], qtt[:],
                                 start=False, stop=True, sem=("t" if half == 1 else None))
                        tmo = p.cur("t")
                        p.wait("vector", "t", tmo)
                        for half in range(2):
                            vg_ = 2 * hd + half
                            evo = p.op("vector", "tensor_copy",
                                       ol[vg_][:, cb:cb + 32],
                                       banko[:, half * 32:half * 32 + 32])
                        banks.set_guard(bio, "v", evo)
                    bi, bank = banks.take("tensor")
                    p.wait("tensor", "v", p.cur("v"))
                    p.op("tensor", "matmul", bank[:, 0:DV],
                         kot[:, hd * 128:(hd + 1) * 128],
                         vt[tt][:, hd * 256:(hd + 1) * 256], start=True, stop=True,
                         sem="t")
                    tds = p.cur("t")
                    va = p.op("vector", "tensor_scalar", e1[:, 33:34],
                              cumP[:, c0 + tb0 + 128:c0 + tb0 + 129],
                              cumP[:, c0 + tb0:c0 + tb0 + 1], None, ALU.subtract)
                    p.wait("scalar", "v", va)
                    se = p.op("scalar", "activation", e2[:, 33:34], e1[:, 33:34], AF.Exp)
                    p.wait("vector", "s", se)
                    p.op("vector", "tensor_scalar", b["Hs"][1][:, 0:DV], S[cur][hd][:],
                         e2[:, 33:34], None, ALU.mult)
                    p.wait("vector", "t", tds)
                    ev = p.op("vector", "tensor_tensor", S[nxt][hd][:],
                              b["Hs"][1][:, 0:DV], bank[:, 0:DV], ALU.add)
                    banks.set_guard(bi, "v", ev)
                    cur, nxt = nxt, cur
            pcur[0] = cur
            vol = p.cur("v")
            p.wait("gpsimd", "v", vol)
            for g in range(G):
                p.dma("gpsimd", o_st[g][:, ts], ol[g][:], "dol")
            vln[0] = p.cur("v")
        vf = p.cur("v")
        p.wait("gpsimd", "v", vf)
        for h in range(NH):
            p.dma("gpsimd", s_half[h], S[pcur[0]][h][:], "dsf")
        p.wait("gpsimd", "dsf", p.cur("dsf"))
        p.wait("gpsimd", "dqg", p.cur("dqg"))
        p.wait("gpsimd", "dsg", p.cur("dsg"))
        p.wait("gpsimd", "dol", p.cur("dol"))
        p.build(es)
    return nc



def build_l2a2():
    nc = bass.Bass("TRN2", target_bir_lowering=False)
    NH, DV = 4, 256
    xin = nc.dram_tensor("xin", [G, 128, T], F32, kind="ExternalInput").ap()
    o_st = nc.dram_tensor("o_st", [G, 128, T], BF16, kind="ExternalInput").ap()
    qg_st = nc.dram_tensor("qg_st", [4, 128, T], BF16, kind="ExternalInput").ap()
    sg_st = nc.dram_tensor("sg_st", [G, 128, T], BF16, kind="ExternalInput").ap()
    s_prev = nc.dram_tensor("s_prev", [4, 128, DV], F32, kind="ExternalInput").ap()
    w_o = nc.dram_tensor("w_o", [128, G, G, 128], F32, kind="ExternalInput").ap()
    gn_w = nc.dram_tensor("gn_w", [128, G], F32, kind="ExternalInput").ap()
    gn_b = nc.dram_tensor("gn_b", [128, G], F32, kind="ExternalInput").ap()
    xout = nc.dram_tensor("xout", [G, 128, T], F32, kind="ExternalOutput").ap()
    GN_EPS = 1e-5

    es = ExitStack()
    with es:
        p = Prog(nc)
        b = alloc_common(nc, es, want_X=False, n_hs=3)
        WO = es.enter_context(nc.sbuf_tensor("WOl2", [128, G * G * 128], BF16))
        gnw = es.enter_context(nc.sbuf_tensor("gnw", [128, G], F32))
        gnb = es.enter_context(nc.sbuf_tensor("gnb", [128, G], F32))
        Spb = [es.enter_context(nc.sbuf_tensor(f"Spb{h}", [128, DV], BF16)) for h in range(4)]
        srot = es.enter_context(nc.sbuf_tensor("srotA", [128, DV], F32))
        Xs = [es.enter_context(nc.sbuf_tensor(f"Xs{g}", [128, SLAB], F32)) for g in range(G)]
        QG = [es.enter_context(nc.sbuf_tensor(f"QG{h}", [128, SLAB], BF16)) for h in range(4)]
        OL = [es.enter_context(nc.sbuf_tensor(f"OL{g}", [128, SLAB], BF16)) for g in range(G)]
        OF = [es.enter_context(nc.sbuf_tensor(f"OF{g}", [128, SLAB], F32)) for g in range(G)]
        SG = [es.enter_context(nc.sbuf_tensor(f"SG{g}", [128, SLAB], BF16)) for g in range(G)]
        ob = [es.enter_context(nc.sbuf_tensor(f"obl2{g}", [128, SLAB], BF16)) for g in range(G)]
        ofb = [es.enter_context(nc.sbuf_tensor(f"ofb{i}", [128, SLAB], BF16)) for i in range(2)]
        osq = [es.enter_context(nc.sbuf_tensor(f"osq{i}", [128, SLAB], BF16)) for i in range(2)]
        Xn = [es.enter_context(nc.sbuf_tensor(f"Xn{i}", [128, SLAB], F32)) for i in range(2)]

        p.op("vector", "memset", b["onesb"][:], 1.0)
        p.op("vector", "memset", b["epsc"][:], GN_EPS)
        p.dma("sync", gnw[:], gn_w[:, :], "dsm")
        p.dma("sync", gnb[:], gn_b[:, :], "dsm")
        p.wait("vector", "dsm", p.cur("dsm"))
        for h in range(4):
            p.dma("sync", srot[:], s_prev[h], "dsp")
            p.wait("vector", "dsp", p.cur("dsp"))
            vcp = p.op("vector", "tensor_copy", Spb[h][:], srot[:])
            p.wait("sync", "v", vcp)
        wl = WeightLoader(nc, es, p, "w")
        wdone = wl.load(w_o.rearrange("p a b c -> p (a b c)"), WO, 0, G * G * 128)

        banks = Banks(p, b["psum"], range(2, 8))
        xo = []
        vsl = [0]
        for s in range(NSLAB):
            ts = slice(s * SLAB, (s + 1) * SLAB)
            p.wait("sync", "v", vsl[0])
            for g in range(G):
                p.dma("sync", Xs[g][:], xin[g][:, ts], "dld")
                p.dma("sync", OL[g][:], o_st[g][:, ts], "dld")
                p.dma("sync", SG[g][:], sg_st[g][:, ts], "dld")
            for h in range(4):
                p.dma("sync", QG[h][:], qg_st[h][:, ts], "dld")
            dl = p.cur("dld")
            p.wait("tensor", "dld", dl)
            if s == 0:
                p.wait("tensor", "v", wdone)
            for h in range(4):
                for half in range(2):
                    vg = 2 * h + half
                    bi, bank = banks.take("tensor")
                    p.op("tensor", "matmul", bank[:], Spb[h][:, half * 128:half * 128 + 128],
                         QG[h][:], start=True, stop=True, sem="t")
                    tmm = p.cur("t")
                    p.wait("vector", "t", tmm)
                    p.wait("vector", "dld", dl)
                    ev = p.op("vector", "tensor_tensor", OF[vg][:], OL[vg][:], bank[:],
                              ALU.add)
                    banks.set_guard(bi, "v", ev)
            psA, psB = b["psum"][0], b["psum"][1]
            for h in range(4):
                for half in range(2):
                    vg = 2 * h + half
                    p.op("vector", "tensor_copy", ofb[half][:], OF[vg][:])
                sq0 = []
                for half in range(2):
                    sv = p.cur("v")
                    p.wait("scalar", "v", sv)
                    sq0.append(p.op("scalar", "activation", osq[half][:], ofb[half][:],
                                    AF.Square))
                vv = p.cur("v")
                p.wait("tensor", "v", vv)
                p.wait("tensor", "s", sq0[-1])
                for half in range(2):
                    p.op("tensor", "matmul", psA[:], b["onesb"][:], ofb[half][:],
                         start=(half == 0), stop=(half == 1), sem=("t" if half else None))
                for half in range(2):
                    p.op("tensor", "matmul", psB[:], b["onesb"][:], osq[half][:],
                         start=(half == 0), stop=(half == 1), sem=("t" if half else None))
                tst = p.cur("t")
                m, var, rstd = b["stat"]
                p.wait("vector", "t", tst)
                p.op("vector", "tensor_scalar", m[:], psA[:], 1.0 / DV, None, ALU.mult)
                p.op("vector", "tensor_tensor", b["Hs"][0][:], m[:], m[:], ALU.mult)
                vvar = p.op("vector", "scalar_tensor_tensor", var[:], psB[:], 1.0 / DV,
                            b["Hs"][0][:], ALU.mult, ALU.subtract)
                p.wait("scalar", "v", vvar)
                p.op("scalar", "activation", b["Hs"][0][:], var[:], AF.Ln,
                     bias=b["epsc"][:, 0:1])
                srstd = p.op("scalar", "activation", rstd[:], b["Hs"][0][:], AF.Exp,
                             scale=-0.5)
                p.wait("vector", "s", srstd)
                for half in range(2):
                    vg = 2 * h + half
                    p.op("vector", "tensor_tensor", b["Hs"][1][:], OF[vg][:], m[:],
                         ALU.subtract)
                    p.op("vector", "tensor_tensor", b["Hs"][2][:], b["Hs"][1][:], rstd[:],
                         ALU.mult)
                    p.op("vector", "tensor_scalar", b["Hs"][1][:], b["Hs"][2][:],
                         gnw[:, vg:vg + 1], gnb[:, vg:vg + 1], ALU.mult, ALU.add)
                    p.op("vector", "tensor_tensor", ob[vg][:], b["Hs"][1][:], SG[vg][:],
                         ALU.mult)
            vob = p.cur("v")
            p.wait("tensor", "v", vob)
            for dg in range(G):
                bi, bank = banks.take("tensor")
                for k in range(G):
                    p.op("tensor", "matmul", bank[:],
                         WO[:, (dg * G + k) * 128:(dg * G + k + 1) * 128], ob[k][:],
                         start=(k == 0), stop=(k == G - 1), sem=("t" if k == G - 1 else None))
                tmm = p.cur("t")
                p.wait("vector", "t", tmm)
                xnp = len(xo) % 2
                if len(xo) >= 2:
                    p.wait("vector", f"dso{xnp}", xo[-2])
                ev = p.op("vector", "tensor_tensor", Xn[xnp][:], bank[:], Xs[dg][:], ALU.add)
                banks.set_guard(bi, "v", ev)
                p.wait("gpsimd", "v", ev)
                xo.append(p.dma("gpsimd", xout[dg][:, ts], Xn[xnp][:], f"dso{xnp}"))
            vsl[0] = p.cur("v")
        p.wait_all("gpsimd", "dso", 2)
        p.build(es)
    return nc



_NC_CACHE = {}


def _shard_inputs(L0_attn_x):
    xs = []
    xa = np.asarray(L0_attn_x)
    for c in range(N_CORES):
        bb, hh = c // 2, c % 2
        xs.append(x_fm(xa[bb, hh * T:(hh + 1) * T, :]))
    return xs


def _route_tail(xs):
    tails, masks = [], []
    for c in range(N_CORES):
        if c % 2 == 0:
            tails.append(np.zeros((128, G), np.float32))
            masks.append(np.zeros((128, 1), np.float32))
        else:
            prev = np.asarray(xs[c - 1])
            tails.append(np.ascontiguousarray(prev[:, :, T - 1].T))
            masks.append(np.ones((128, 1), np.float32))
    return tails, masks


def run_ffn01(xs, Wr, Wk, Wv, mur, muk):
    nc = _NC_CACHE.get("ffn01")
    if nc is None:
        nc = _NC_CACHE.setdefault("ffn01", build_ffn01())
    tails, masks = _route_tail(xs)
    pr, pk, pv = pack_w(Wr), pack_w(Wk), pack_w(Wv)
    vr, vk = vec_fm(mur), vec_fm(muk)
    in_maps = [{"xin": xs[c], "xtail": tails[c], "w_r": pr, "w_k": pk, "w_v": pv,
                "mu_r": vr, "mu_k": vk, "has_tail": masks[c]} for c in range(N_CORES)]
    res = run_bass_kernel_spmd(nc, in_maps, core_ids=list(range(N_CORES)))
    return [res.results[c]["xout"] for c in range(N_CORES)]


def pack_ai(A):
    return np.ascontiguousarray(np.asarray(A).reshape(G, 128, 8).transpose(1, 0, 2)).astype(np.float32)


def pack_bi(B):
    return np.ascontiguousarray(np.asarray(B).reshape(8, G, 128)).astype(np.float32)


def pos_bcast(half):
    pos = np.arange(1, 4096 + 1, dtype=np.float64)[half * T:(half + 1) * T]
    row = (1.0 / (1.0 + 0.1 * np.log(pos))).astype(np.float32)
    return np.ascontiguousarray(np.broadcast_to(row, (128, T)))


def run_l0a(xs, ii):
    nc1 = _NC_CACHE.get("l0a1") or _NC_CACHE.setdefault("l0a1", build_l0a1())
    pw = {k: pack_w(ii[f"L0_attn_W{k}"]) for k in "qkvgo"}
    p0 = pack_lhsT(np.asarray(ii["L0_attn_W0"]))
    pa, pb = pack_ai(ii["L0_attn_A_init"]), pack_bi(ii["L0_attn_B_init"])
    lng, lnb = vec_fm(ii["L0_attn_ln_g"]), vec_fm(ii["L0_attn_ln_b"])
    llr, lsg = vec_fm(ii["L0_attn_log_lr"]), vec_fm(ii["L0_attn_ls_g"])
    in_maps = [{"xin": xs[c], "w_q": pw["q"], "w_k": pw["k"], "w_v": pw["v"],
                "w_0": p0, "a_i": pa, "b_i": pb, "ln_g": lng, "ln_b": lnb}
               for c in range(N_CORES)]
    r1 = run_bass_kernel_spmd(nc1, in_maps, core_ids=list(range(N_CORES)))
    nc2 = _NC_CACHE.get("l0a2") or _NC_CACHE.setdefault("l0a2", build_l0a2())
    zg = np.zeros((128, G), np.float32)
    in_maps2 = []
    for c in range(N_CORES):
        gp = zg if c % 2 == 0 else np.asarray(r1.results[c - 1]["gsum"])
        in_maps2.append({"xin": xs[c], "q_st": np.asarray(r1.results[c]["q_st"]),
                         "y0_st": np.asarray(r1.results[c]["y0_st"]),
                         "cum_st": np.asarray(r1.results[c]["cum_st"]),
                         "gprev": gp, "w_g": pw["g"], "w_o": pw["o"],
                         "log_lr": llr, "ls_g": lsg, "posb": pos_bcast(c % 2)})
    r2 = run_bass_kernel_spmd(nc2, in_maps2, core_ids=list(range(N_CORES)))
    return [r2.results[c]["xout"] for c in range(N_CORES)]


def run_l1a(xs, ii):
    nc = _NC_CACHE.get("l1a") or _NC_CACHE.setdefault("l1a", build_l1a())
    pq, pk = pack_lhsT(ii["L1_attn_wq"]), pack_lhsT(ii["L1_attn_wk"])
    pvt, po = pack_rhs(ii["L1_attn_wv"]), pack_lhsT(ii["L1_attn_wo"])
    bq, bk, bo = (vec_fm(ii[f"L1_attn_b{k}"]) for k in "qko")
    bvb = np.ascontiguousarray(np.broadcast_to(
        np.asarray(ii["L1_attn_bv"], np.float32), (128, C)))
    idx = np.arange(128)
    tri1 = (idx[:, None] >= idx[None, :]).astype(np.float32)
    tri2 = (idx[:, None] <= idx[None, :]).astype(np.float32)
    ones_m = np.ones((128, 1), np.float32)
    zero_m = np.zeros((128, 1), np.float32)
    zedge = np.zeros((G, 128, 128), np.float32)
    in_maps = []
    for c in range(N_CORES):
        if c % 2 == 0:
            xl, mL = zedge, zero_m
            xr = np.ascontiguousarray(np.asarray(xs[c + 1])[:, :, 0:128])
            mR = ones_m
        else:
            xl = np.ascontiguousarray(np.asarray(xs[c - 1])[:, :, T - 128:T])
            mL = ones_m
            xr, mR = zedge, zero_m
        in_maps.append({"xin": xs[c], "xeL": xl, "xeR": xr, "mskL": mL, "mskR": mR,
                        "w_q": pq, "w_k": pk, "w_vt": pvt, "w_o": po,
                        "b_q": bq, "b_k": bk, "b_vt": bvb, "b_o": bo,
                        "tri1": tri1, "tri2": tri2})
    res = run_bass_kernel_spmd(nc, in_maps, core_ids=list(range(N_CORES)))
    return [res.results[c]["xout"] for c in range(N_CORES)]


def run_l2f(xs, ii):
    nc = _NC_CACHE.get("l2f") or _NC_CACHE.setdefault("l2f", build_l2f())
    pf, pp = pack_w(ii["L2_ffn_W_fc"]), pack_w(ii["L2_ffn_W_proj"])
    in_maps = [{"xin": xs[c], "w_fc": pf, "w_pr": pp} for c in range(N_CORES)]
    res = run_bass_kernel_spmd(nc, in_maps, core_ids=list(range(N_CORES)))
    return [res.results[c]["xout"] for c in range(N_CORES)]


def run_l3f(xs, ii):
    nc = _NC_CACHE.get("l3f") or _NC_CACHE.setdefault("l3f", build_l3f())
    pw = {"w_1": pack_w(ii["L3_ffn_w1_w"]), "w_2": pack_w(ii["L3_ffn_w2_w"]),
          "h_1": pack_w(ii["L3_ffn_h1_w"]), "h_2": pack_w(ii["L3_ffn_h2_w"]),
          "w_3": pack_w(ii["L3_ffn_w3_w"]),
          "s_1": vec_fm(ii["L3_ffn_w1_s"]), "s_2": vec_fm(ii["L3_ffn_w2_s"]),
          "s_h1": vec_fm(ii["L3_ffn_h1_s"]), "s_h2": vec_fm(ii["L3_ffn_h2_s"]),
          "s_3": vec_fm(ii["L3_ffn_w3_s"])}
    in_maps = [dict(pw, xin=xs[c]) for c in range(N_CORES)]
    res = run_bass_kernel_spmd(nc, in_maps, core_ids=list(range(N_CORES)))
    return [res.results[c]["xout"] for c in range(N_CORES)]


def run_l3a(xs, ii):
    nc1 = _NC_CACHE.get("l3a1") or _NC_CACHE.setdefault("l3a1", build_l3a1())
    pq, pk = pack_w(ii["L3_attn_w_q"]), pack_w(ii["L3_attn_w_k"])
    pvt = pack_rhs(np.ascontiguousarray(np.asarray(ii["L3_attn_w_v"]).T))
    idm = np.eye(128, dtype=np.float32)
    idx = np.arange(128)[:, None]
    cols = np.arange(256)[None, :]
    bcm = np.stack([(j * 128 + idx <= cols).astype(np.float32) for j in range(2)])
    in_maps = [{"xin": xs[c], "w_q": pq, "w_k": pk, "w_vt": pvt, "idm": idm, "bcm": bcm}
               for c in range(N_CORES)]
    r1 = run_bass_kernel_spmd(nc1, in_maps, core_ids=list(range(N_CORES)))
    nc2 = _NC_CACHE.get("l3a2") or _NC_CACHE.setdefault("l3a2", build_l3a2())
    pm = pack_lhsT(np.asarray(ii["L3_attn_memory"]))
    po = pack_w(ii["L3_attn_w_o"])
    mn = np.asarray(ii["L3_attn_memory_norm"], np.float32)
    mn_b = np.ascontiguousarray(np.broadcast_to(
        mn.reshape(G, 128).transpose(1, 0)[:, :, None], (128, G, 128))).astype(np.float32)
    mn_v = vec_fm(mn)
    gate_b = np.full((128, 1), float(np.asarray(ii["L3_attn_gate"]).reshape(-1)[0]), np.float32)
    zs = np.zeros((G, 128, C + 1), np.float32)
    in_maps2 = []
    for c in range(N_CORES):
        sp = zs if c % 2 == 0 else np.asarray(r1.results[c - 1]["s_out"])
        in_maps2.append({"xin": xs[c], "s_prev": sp,
                         "sq_st": np.asarray(r1.results[c]["sq_st"]),
                         "o_st": np.asarray(r1.results[c]["o_st"]),
                         "den_st": np.asarray(r1.results[c]["den_st"]),
                         "w_m": pm, "mn_b": mn_b, "mn_v": mn_v, "w_o": po,
                         "gate_b": gate_b})
    r2 = run_bass_kernel_spmd(nc2, in_maps2, core_ids=list(range(N_CORES)))
    return [r2.results[c]["xout"] for c in range(N_CORES)]


def run_l2a(xs, ii):
    nc1 = _NC_CACHE.get("l2a1") or _NC_CACHE.setdefault("l2a1", build_l2a1())
    Wx0 = np.asarray(ii["L2_attn_W_x0"], np.float32)
    Wx0p = np.zeros((256, 1024), np.float32)
    Wx0p[0:96] = Wx0[0:96]
    Wx0p[128:192] = Wx0[96:160]
    Wx2 = np.asarray(ii["L2_attn_W_x2"], np.float32)
    wx2p = np.ascontiguousarray(
        Wx2.reshape(G, 128, 5, 32).transpose(3, 2, 0, 1)).astype(np.float32)
    Ww1 = np.zeros((128, 1024), np.float32)
    Ww1[0:64] = np.asarray(ii["L2_attn_W_w1"], np.float32)
    Ww2 = np.asarray(ii["L2_attn_W_w2"], np.float32)
    ww2p = np.ascontiguousarray(Ww2.T.reshape(64, 4, 128)).astype(np.float32)
    xb5 = np.concatenate([vec_fm(np.asarray(ii["L2_attn_x_bias"])[n]) for n in range(5)],
                         axis=1)
    idm = np.eye(128, dtype=np.float32)
    sI = np.arange(128)[:, None]
    iI = np.arange(32)[None, :]
    m1 = np.stack([(sI < ck * 32 + iI).astype(np.float32) for ck in range(4)])
    m2 = np.stack([(sI == ck * 32 + iI).astype(np.float32) for ck in range(4)])
    pw = {"w_x0": pack_w(Wx0p), "w_x2": wx2p, "mu_0": vec_fm(ii["L2_attn_mu0"]),
          "x_b": xb5, "w_r": pack_w(ii["L2_attn_W_r"]), "w_w1": pack_w(Ww1),
          "w_w2": ww2p, "b_w2": vec_fm(ii["L2_attn_b_w2"]),
          "w_k": pack_w(ii["L2_attn_W_k"]),
          "w_vt": pack_rhs(np.ascontiguousarray(np.asarray(ii["L2_attn_W_v"]).T)),
          "w_g": pack_w(ii["L2_attn_W_g"]),
          "u_h": np.ascontiguousarray(np.asarray(ii["L2_attn_bonus"], np.float32).T),
          "idm": idm, "msk1": m1, "msk2": m2}
    tails, masks = _route_tail(xs)
    in_maps = [dict(pw, xin=xs[c], xtail=tails[c], has_tail=masks[c])
               for c in range(N_CORES)]
    r1 = run_bass_kernel_spmd(nc1, in_maps, core_ids=list(range(N_CORES)))
    nc2 = _NC_CACHE.get("l2a2") or _NC_CACHE.setdefault("l2a2", build_l2a2())
    po = pack_w(ii["L2_attn_W_o"])
    gnwv, gnbv = vec_fm(ii["L2_attn_gn_weight"]), vec_fm(ii["L2_attn_gn_bias"])
    zs = np.zeros((4, 128, 256), np.float32)
    in_maps2 = []
    for c in range(N_CORES):
        sp = zs if c % 2 == 0 else np.asarray(r1.results[c - 1]["s_half"])
        in_maps2.append({"xin": xs[c], "o_st": np.asarray(r1.results[c]["o_st"]),
                         "qg_st": np.asarray(r1.results[c]["qg_st"]),
                         "sg_st": np.asarray(r1.results[c]["sg_st"]),
                         "s_prev": sp, "w_o": po, "gn_w": gnwv, "gn_b": gnbv})
    r2 = run_bass_kernel_spmd(nc2, in_maps2, core_ids=list(range(N_CORES)))
    return [r2.results[c]["xout"] for c in range(N_CORES)]


def kernel(**inputs):
    xs = _shard_inputs(inputs["L0_attn_x"])
    L0f = [inputs[k] for k in ("L0_ffn_W_r", "L0_ffn_W_k", "L0_ffn_W_v",
                               "L0_ffn_mu_r", "L0_ffn_mu_k")]
    L1f = [inputs[k] for k in ("L1_ffn_W_r", "L1_ffn_W_k", "L1_ffn_W_v",
                               "L1_ffn_mu_r", "L1_ffn_mu_k")]
    xs = run_ffn01(xs, *L0f)
    xs = run_l0a(xs, inputs)
    xs = run_ffn01(xs, *L0f)
    xs = run_ffn01(xs, *L1f)
    xs = run_l1a(xs, inputs)
    xs = run_ffn01(xs, *L1f)
    xs = run_l2f(xs, inputs)
    xs = run_l2a(xs, inputs)
    xs = run_l2f(xs, inputs)
    xs = run_l3f(xs, inputs)
    xs = run_l3a(xs, inputs)
    xs = run_l3f(xs, inputs)
    out = np.zeros((4, 4096, C), np.float32)
    for c in range(N_CORES):
        bb, hh = c // 2, c % 2
        out[bb, hh * T:(hh + 1) * T, :] = x_unfm(xs[c])
    return out
